# version 22
# baseline (speedup 1.0000x reference)
import numpy as np
import ml_dtypes
import concourse.bass as bass
import concourse.mybir as mybir
from concourse.bass_utils import run_bass_kernel_spmd
from concourse.alu_op_type import AluOpType as op
from contextlib import ExitStack

F32 = mybir.dt.float32
BF16 = mybir.dt.bfloat16
I32 = mybir.dt.int32
AF = mybir.ActivationFunctionType
AX = mybir.AxisListType.X

P = 128
NT = 32
D = 1024
HID = 2048
IFF = 2816
C32 = float(np.float32(np.float64(1024.0) ** 0.5 + 1e-6))


class Sched:
    def __init__(self):
        self.ops = {k: [] for k in ("sy", "gp", "v", "sc", "pe")}
        self.cnt = {"SY": 0, "GP": 0, "ST": 0, "V": 0, "SC": 0, "PE": 0}
        self.ev = {}
        self.evsem = {}

    def add(self, eng, fn, waits=(), inc=None, ev=None):
        w = []
        for item in waits:
            if item is None:
                continue
            if isinstance(item, str):
                w.append((self.evsem[item], self.ev[item]))
            else:
                sm, val = item
                if val > 0:
                    w.append((sm, val))
        amt = 0
        if inc is not None:
            amt = 1 if inc in ("V", "SC", "PE") else 16
            if inc not in self.cnt:
                self.cnt[inc] = 0
            self.cnt[inc] += amt
        self.ops[eng].append((fn, w, inc, amt))
        if ev is not None:
            assert inc is not None
            self.ev[ev] = self.cnt[inc]
            self.evsem[ev] = inc

    def gev(self, name, sem):
        return (self.evsem.get(name, sem), self.ev.get(name, 0))

    def mark(self, name):
        self.marks = getattr(self, "marks", {})
        self.marks[name] = ({k: len(v) for k, v in self.ops.items()}, dict(self.cnt))


S = Sched()


def v_drain():
    S.add("v", lambda v: v.drain())


def rsqrt_nr(ap_in, ap_out, ti, t1, t2, t3, waits=(), ev=None):
    S.add("v", lambda v: v.tensor_scalar(ti, ap_in.bitcast(I32), 17, 0x4B000000,
                                         op.logical_shift_right, op.bitwise_or),
          waits=list(waits))
    v_drain()
    S.add("v", lambda v: v.tensor_scalar(t1, ti.bitcast(F32), -1.0,
                                         float(0x1000000 + 0x5F37), op.mult, op.add))
    v_drain()
    S.add("v", lambda v: v.tensor_scalar(ti, t1.bitcast(I32), 0x7FFFFF, 16,
                                         op.bitwise_and, op.logical_shift_left))
    v_drain()
    yv = ti.bitcast(F32)
    for it in range(4):
        last = it == 3
        S.add("v", lambda v, yv=yv: v.tensor_tensor(t2, yv, yv, op.mult))
        v_drain()
        S.add("v", lambda v: v.tensor_tensor(t3, ap_in, t2, op.mult))
        v_drain()
        S.add("v", lambda v: v.tensor_scalar(t2, t3, -0.5, 1.5, op.mult, op.add))
        v_drain()
        dst = ap_out if last else t1
        S.add("v", lambda v, yv=yv, dst=dst: v.tensor_tensor(dst, yv, t2, op.mult),
              inc="V" if last else None, ev=ev if last else None)
        if not last:
            v_drain()
            yv = t1


def recip_nr(ap_in, ap_out, ti, t1, t2, waits=(), ev=None):
    S.add("v", lambda v: v.tensor_scalar(ti, ap_in.bitcast(I32), 16, 0x4B000000,
                                         op.logical_shift_right, op.bitwise_or),
          waits=list(waits))
    v_drain()
    S.add("v", lambda v: v.tensor_scalar(t1, ti.bitcast(F32), -1.0,
                                         float(0x1000000 + 0x7EF1), op.mult, op.add))
    v_drain()
    S.add("v", lambda v: v.tensor_scalar(ti, t1.bitcast(I32), 0x7FFFFF, 16,
                                         op.bitwise_and, op.logical_shift_left))
    v_drain()
    yv = ti.bitcast(F32)
    for it in range(4):
        last = it == 3
        S.add("v", lambda v, yv=yv: v.tensor_tensor(t2, ap_in, yv, op.mult))
        v_drain()
        S.add("v", lambda v: v.tensor_scalar(t2, t2, -1.0, 2.0, op.mult, op.add))
        v_drain()
        dst = ap_out if last else t1
        S.add("v", lambda v, yv=yv, dst=dst: v.tensor_tensor(dst, yv, t2, op.mult),
              inc="V" if last else None, ev=ev if last else None)
        if not last:
            v_drain()
            yv = t1


def build():
    global S
    S = Sched()
    nc = bass.Bass("TRN2", target_bir_lowering=False)
    g = {}
    g["x"] = nc.dram_tensor("x", [NT * P, D], F32, kind="ExternalInput").ap()
    for nm, shp in [("L0_attn_scale1", [D]), ("L0_attn_scale2", [D]), ("L0_attn_gamma", [D]),
                    ("L0_attn_qkv_w", [3 * D, D]), ("L0_attn_proj_w", [D, D]),
                    ("L0_attn_proj_b", [D]), ("L0_attn_conv1_w", [HID, D]),
                    ("L0_attn_conv1_b", [HID]), ("L0_attn_conv2_w", [D, HID]),
                    ("L0_attn_conv2_b", [D]), ("L0_ffn_w_gate", [IFF, D]),
                    ("L0_ffn_w_up", [IFF, D]), ("L0_ffn_w_down", [D, IFF])]:
        g[nm] = nc.dram_tensor(nm, shp, F32, kind="ExternalInput").ap()
    g["out"] = nc.dram_tensor("out", [NT * P, D], F32, kind="ExternalOutput").ap()
    import os as _os
    _dbg = "ExternalOutput" if _os.environ.get("KDBG") else "Internal"
    for nm, shp in [("qkv_bf", [3 * D, D]), ("proj_bf", [D, D]), ("c1_bf", [HID, D]),
                    ("c2_bf", [D, HID]), ("wqg_h", [IFF, D]), ("wqu_h", [IFF, D]),
                    ("wqd_h", [D, IFF]), ("x2_h", [NT * P, D]), ("s_h", [NT * P, D]),
                    ("xqT_h", [NT * P, D]), ("hqT_h", [NT * P, IFF])]:
        g[nm] = nc.dram_tensor(nm, shp, BF16, kind=_dbg).ap()
    g["gate_h"] = nc.dram_tensor("gate_h", [NT * P, IFF], F32, kind=_dbg).ap()
    g["dbg"] = nc.dram_tensor("dbg", [P, 6 * NT], F32, kind="ExternalOutput").ap()

    with ExitStack() as es:
        ec = es.enter_context
        sems = {k: ec(nc.semaphore(k)) for k in ("SY", "GP", "ST", "V", "SC", "PE")}

        midstack = ExitStack()
        em = midstack.enter_context
        ident_b = ec(nc.sbuf_tensor("ident_b", [P, P], BF16))
        it_i = ec(nc.sbuf_tensor("it_i", [P, P], I32))
        zrow = ec(nc.sbuf_tensor("zrow", [1, 512], BF16))
        thrB = ec(nc.sbuf_tensor("thrB", [P, 3], F32))
        s1_all = ec(nc.sbuf_tensor("s1_all", [P, NT], F32))
        s2_all = ec(nc.sbuf_tensor("s2_all", [P, NT], F32))
        rstd_all = ec(nc.sbuf_tensor("rstd_all", [P, NT], F32))
        mrs_all = ec(nc.sbuf_tensor("mrs_all", [P, NT], F32))
        rstd1_all = ec(nc.sbuf_tensor("rstd1_all", [P, NT], F32))
        ssq2_all = ec(nc.sbuf_tensor("ssq2_all", [P, NT], F32))
        rstd2_all = ec(nc.sbuf_tensor("rstd2_all", [P, NT], F32))
        st1 = ec(nc.sbuf_tensor("st1", [P, NT], F32))
        st2 = ec(nc.sbuf_tensor("st2", [P, NT], F32))
        st3 = ec(nc.sbuf_tensor("st3", [P, NT], F32))
        st4 = ec(nc.sbuf_tensor("st4", [P, NT], F32))
        st5 = ec(nc.sbuf_tensor("st5", [P, NT], F32))
        sti = ec(nc.sbuf_tensor("sti", [P, NT], I32))
        xa = [ec(nc.sbuf_tensor("xa%d" % i, [P, D], F32)) for i in range(2)]
        xn_t = [ec(nc.sbuf_tensor("xn%d" % i, [P, D], F32)) for i in range(2)]
        reps = {nm: em(nc.sbuf_tensor("rep_" + nm, [P, D], F32))
                for nm in ("s1", "s2", "gm", "pb", "c2b")}
        c1b_fm = em(nc.sbuf_tensor("c1b_fm", [P, HID // P], F32))

        for nm, src, dst, csem in [("qkv", "L0_attn_qkv_w", "qkv_bf", "CQ"),
                                   ("proj", "L0_attn_proj_w", "proj_bf", "CP"),
                                   ("c1", "L0_attn_conv1_w", "c1_bf", "C1"),
                                   ("c2", "L0_attn_conv2_w", "c2_bf", "C2")]:
            S.add("gp", lambda g_, s=src, d=dst: g_.dma_start(out=g[d][:, :], in_=g[s][:, :]),
                  inc=csem, ev="cast_" + nm)
        S.add("gp", lambda g_: g_.memset(thrB[:, 0:1], -6.25), inc="GP")
        S.add("gp", lambda g_: g_.memset(thrB[:, 1:2], -12.25), inc="GP")
        S.add("gp", lambda g_: g_.memset(thrB[:, 2:3], -25.0), inc="GP", ev="thrB")
        S.add("gp", lambda g_: g_.iota(out=it_i[:, :], pattern=[[1, P]], base=0,
                                       channel_multiplier=-1), inc="GP", ev="iota_id")
        S.add("gp", lambda g_: g_.memset(zrow[:], 0.0), inc="GP", ev="zrow")
        S.add("v", lambda v: v.tensor_scalar(ident_b[:, :], it_i[:, :], 0, None,
                                             op.is_equal),
              waits=["iota_id"], inc="V", ev="identb")
        for i, nm in enumerate(("L0_attn_scale1", "L0_attn_scale2", "L0_attn_gamma",
                                "L0_attn_proj_b", "L0_attn_conv2_b")):
            S.add("sy", lambda sy, i=i, nm=nm: sy.dma_start(
                out=vrows[i][:], in_=g[nm].rearrange("d -> () d")), inc="VR")
        for oc in range(HID // P):
            S.add("sy", lambda sy, oc=oc: sy.dma_start(
                out=c1b_fm[:, oc:oc + 1],
                in_=g["L0_attn_conv1_b"][P * oc:P * (oc + 1)].rearrange("p -> p ()")),
                inc="VR", ev=("ld_vrows" if oc == HID // P - 1 else None))
        S.add("v", lambda v: v.memset(ones1[:], 1.0), inc="V", ev="ones1")
        with ExitStack() as p0:
            vrows = [p0.enter_context(nc.sbuf_tensor("vrow%d" % i, [1, D], F32)) for i in range(5)]
            ones1 = p0.enter_context(nc.sbuf_tensor("ones1", [1, P], F32))
            rep_ps = p0.enter_context(nc.psum_tensor("rep_ps", [P, 512], F32))
            xsq = [p0.enter_context(nc.sbuf_tensor("xsq%d" % i, [P, D], F32)) for i in range(2)]
            prev = None
            for i, nm in enumerate(("s1", "s2", "gm", "pb", "c2b")):
                for h in range(2):
                    S.add("pe", lambda pe, i=i, h=h: pe.matmul(
                        rep_ps[:], ones1[:], vrows[i][:, 512 * h:512 * (h + 1)],
                        start=True, stop=True),
                        waits=["ones1", "ld_vrows"] + ([prev] if prev else []),
                        inc="PE", ev="repm%d_%d" % (i, h))
                    S.add("v", lambda v, nm=nm, h=h: v.tensor_copy(
                        reps[nm][:, 512 * h:512 * (h + 1)], rep_ps[:]),
                        waits=["repm%d_%d" % (i, h)], inc="V", ev="repc%d_%d" % (i, h))
                    prev = "repc%d_%d" % (i, h)
            for t in range(NT):
                b = t % 2
                S.add("sy", lambda sy, t=t, b=b: sy.dma_start(
                    out=xa[b][:], in_=g["x"][P * t:P * (t + 1), :]),
                    waits=[S.gev("p05v%d" % (t - 2), "V")], inc="XA%d" % b, ev="p05x%d" % t)
                S.add("sc", lambda sc, b=b: sc.activation(xsq[b][:], xa[b][:], AF.Square),
                      waits=["p05x%d" % t, S.gev("p05v%d" % (t - 2), "V")],
                      inc="SC", ev="p05sq%d" % t)
                S.add("v", lambda v, t=t, b=b: v.reduce_sum(s1_all[:, t:t + 1], xa[b][:], AX),
                      waits=["p05x%d" % t], inc="V")
                S.add("v", lambda v, t=t, b=b: v.reduce_sum(s2_all[:, t:t + 1], xsq[b][:], AX),
                      waits=["p05sq%d" % t], inc="V", ev="p05v%d" % t)
            dbg_sb = p0.enter_context(nc.sbuf_tensor("dbg_sb", [P, 6 * NT], F32))
            S.add("v", lambda v: v.tensor_copy(dbg_sb[:, 0:NT], s1_all[:]),
                  waits=["p05v%d" % (NT - 1)], inc="V", ev="dbgc0")
            S.add("v", lambda v: v.tensor_copy(dbg_sb[:, NT:2 * NT], s2_all[:]), inc="V", ev="dbgc1")
            S.add("v", lambda v: v.tensor_scalar(st1[:], s1_all[:], 1.0 / D, None, op.mult))
            v_drain()
            S.add("v", lambda v: v.tensor_tensor(st2[:], st1[:], st1[:], op.mult))
            v_drain()
            S.add("v", lambda v: v.scalar_tensor_tensor(st2[:], s2_all[:], 1.0 / D, st2[:],
                                                        op.mult, op.subtract))
            v_drain()
            S.add("v", lambda v: v.tensor_scalar(st3[:], st2[:], 1.0, 1e-5, op.mult, op.add))
            v_drain()
            rsqrt_nr(st3[:], rstd_all[:], sti[:], st4[:], st5[:], s1_all[:])
            v_drain()
            S.add("v", lambda v: v.scalar_tensor_tensor(mrs_all[:], st1[:], -1.0, rstd_all[:],
                                                        op.mult, op.mult))
            v_drain()
            S.add("v", lambda v: v.tensor_tensor(st1[:], rstd_all[:], rstd_all[:], op.mult))
            v_drain()
            S.add("v", lambda v: v.tensor_scalar(st2[:], st2[:], float(D), None, op.mult))
            v_drain()
            S.add("v", lambda v: v.tensor_tensor(st3[:], st1[:], st2[:], op.mult))
            v_drain()
            rsqrt_nr(st3[:], rstd1_all[:], sti[:], st4[:], st5[:], s1_all[:])
            v_drain()
            S.add("v", lambda v: v.tensor_scalar(rstd1_all[:], rstd1_all[:], C32, None, op.mult),
                  inc="V", ev="stats_done")
            for j, tile in ((2, rstd_all), (3, mrs_all), (4, rstd1_all), (5, st2)):
                S.add("v", lambda v, j=j, tile=tile: v.tensor_copy(
                    dbg_sb[:, NT * j:NT * (j + 1)], tile[:]), waits=["stats_done"],
                    inc="V", ev="dbgc%d" % j)
            S.add("sy", lambda sy: sy.dma_start(out=g["dbg"][:, :], in_=dbg_sb[:]),
                  waits=["dbgc5"], inc="ST", ev="dbg_st")

        S.mark("p0")
        with ExitStack() as p1:
            e1 = p1.enter_context
            qkvT = e1(nc.sbuf_tensor("qkvT", [P, 8 * 3 * D], BF16))
            projT = e1(nc.sbuf_tensor("projT", [P, 8 * D], BF16))
            xn1_t = [e1(nc.sbuf_tensor("xn1%d" % i, [P, D], BF16)) for i in range(2)]
            fm_sup = [e1(nc.sbuf_tensor("fmsup%d" % i, [P, 8 * 512], BF16)) for i in range(2)]
            kphi = [e1(nc.sbuf_tensor("kphi%d" % i, [P, D], BF16)) for i in range(2)]
            vext = [e1(nc.sbuf_tensor("vext%d" % i, [P, 8 * 129], BF16)) for i in range(2)]
            sq_s0 = e1(nc.sbuf_tensor("sqs0", [P, 512], F32))
            sq_s1 = e1(nc.sbuf_tensor("sqs1", [P, 512], F32))
            sq_s = [sq_s0, sq_s1]
            phi_q = e1(nc.sbuf_tensor("phi_q", [P, 8 * 2048], BF16))
            kv_sb = e1(nc.sbuf_tensor("kv_sb", [P, 8 * 129], BF16))
            kv_bd = e1(nc.sbuf_tensor("kv_bd", [P, 8 * P], BF16))
            bd = e1(nc.sbuf_tensor("bd", [P, 16], BF16))
            zq_sb = e1(nc.sbuf_tensor("zq_sb", [P, 256], F32))
            z_tm = e1(nc.sbuf_tensor("z_tm", [P, 256], F32))
            zn1 = e1(nc.sbuf_tensor("zn1", [P, 256], F32))
            zn2 = e1(nc.sbuf_tensor("zn2", [P, 256], F32))
            zni = e1(nc.sbuf_tensor("zni", [P, 256], I32))
            atz = e1(nc.sbuf_tensor("atz", [P, D], BF16))
            atf = e1(nc.sbuf_tensor("atf", [P, D], BF16))
            x2_t = [e1(nc.sbuf_tensor("x2t%d" % i, [P, D], BF16)) for i in range(2)]
            x2w = e1(nc.sbuf_tensor("x2w", [P, D], F32))
            xp_t2 = [e1(nc.psum_tensor("xpt%d" % i, [P, P], BF16)) for i in range(2)]
            xp_ps = [xp_t2[0][:], xp_t2[1][:]]

            for ic in range(8):
                S.add("sy", lambda sy, ic=ic: sy.dma_start_transpose(
                    out=qkvT[:, 3 * D * ic:3 * D * (ic + 1)],
                    in_=g["qkv_bf"][:, P * ic:P * (ic + 1)]),
                    waits=["cast_qkv"] + (["stats_done", "dbg_st"] if ic == 0 else []),
                    inc="TQ", ev="qkvT%d" % ic)
            for ic in range(8):
                S.add("sy", lambda sy, ic=ic: sy.dma_start_transpose(
                    out=projT[:, D * ic:D * (ic + 1)],
                    in_=g["proj_bf"][:, P * ic:P * (ic + 1)]),
                    waits=["cast_proj"], inc="TP", ev="projT%d" % ic)
            for i in range(2):
                S.add("v", lambda v, i=i: v.memset(
                    vext[i][:].rearrange("p (a c) -> p a c", c=129)[:, :, 128:129], 1.0),
                    inc="V", ev="vext_ones%d" % i)
            S.add("v", lambda v: v.memset(kv_bd[:], 0.0), inc="V", ev="kvbd_zero")

            S.mark("p1pro")
            for seq in range(2):
                t0 = seq * 16
                with ExitStack() as sA:
                    eA = sA.enter_context
                    q_ps = [eA(nc.psum_tensor("q_ps%d" % seq, [P, 512], F32))] * 2
                    kvg_ps = [eA(nc.psum_tensor("kvg_ps%d_%d" % (seq, i), [P, 512], F32))
                              for i in range(2)]
                    kvA = eA(nc.psum_tensor("kvA%d" % seq, [P, 3 * 129], F32))
                    kvB = eA(nc.psum_tensor("kvB%d" % seq, [P, 3 * 129], F32))
                    kvCz = eA(nc.psum_tensor("kvCz%d" % seq, [P, 512], F32))
                    kvC = kvCz[:, 0:2 * 129]

                    def kv_slot(p_):
                        if p_ < 3:
                            return kvA[:], 129 * p_
                        if p_ < 6:
                            return kvB[:], 129 * (p_ - 3)
                        return kvC, 129 * (p_ - 6)

                    if seq == 0:
                        zmw = ["zrow", S.gev("repc4_1", "V")]
                    else:
                        zmw = ["b0_kvsb2", "b0_zev15",
                               S.gev("c_atz%d_1" % 15, "V"), S.gev("c_x2%d" % 15, "V")]
                    for zi, (zt, zn) in enumerate(((kvA, 387), (kvB, 387), (kvCz, 512))):
                        S.add("pe", lambda pe, zt=zt, zn=zn: pe.matmul(
                            zt[:, 0:zn], zrow[0:1, 0:P], zrow[0:1, 0:zn],
                            start=True, stop=False),
                            waits=zmw if zi == 0 else [], inc="PE")

                    def a_part1(t):
                        tt_ = t - t0
                        b = t % 2
                        ldw = [S.gev("a_xn%d" % (t - 2), "SC")]
                        if t < 2:
                            ldw.append("stats_done")
                        if seq == 1 and tt_ < 2:
                            ldw.append(S.gev("c_xn%d" % (t0 - 2 + tt_), "SC"))
                        S.add("sy", lambda sy, t=t, b=b: sy.dma_start(
                            out=xa[b][:], in_=g["x"][P * t:P * (t + 1), :]),
                            waits=ldw, inc="XA%d" % b, ev="a_x%d" % t)
                        S.add("sc", lambda sc, t=t, b=b: sc.activation(
                            xn_t[b][:], xa[b][:], AF.Identity,
                            bias=mrs_all[:, t:t + 1], scale=rstd_all[:, t:t + 1]),
                            waits=["a_x%d" % t, S.gev("a_xn1%d" % (t - 2), "V")] +
                                  (["stats_done"] if t < 2 else []),
                            inc="SC", ev="a_xn%d" % t)
                        xw = ["a_xn%d" % t, S.gev("a_fme%d_7" % (t - 2), "SC")]
                        if t == 0:
                            xw.append("repc0_1")
                        if tt_ % 4 == 0:
                            xw.append(S.gev("a_q%d_7" % (t - 5), "PE"))
                        S.add("v", lambda v, t=t, b=b: v.scalar_tensor_tensor(
                            xn1_t[b][:], xn_t[b][:], rstd1_all[:, t:t + 1], reps["s1"][:],
                            op.mult, op.mult),
                            waits=xw, inc="V", ev="a_xn1%d" % t)

                    a_part1(t0)
                    for tt in range(16):
                        t = t0 + tt
                        b = t % 2
                        sp = tt % 4
                        sup = fm_sup[(tt // 4) % 2]
                        for ic in range(8):
                            tw = ["a_xn1%d" % t]
                            if ic < 2:
                                tw.append(S.gev("a_fme%d_%d" % (t - 1, ic + 6), "SC"))
                            else:
                                tw.append("a_fme%d_%d" % (t, ic - 2))
                            if t == 0 and ic < 2:
                                tw.append("identb")
                            S.add("pe", lambda pe, b=b, ic=ic: pe.transpose(
                                xp_ps[ic % 2], xn1_t[b][:, P * ic:P * (ic + 1)], ident_b[:]),
                                waits=tw, inc="PE", ev="a_tp%d_%d" % (t, ic))
                            S.add("sc", lambda sc, sup=sup, sp=sp, ic=ic: sc.copy(
                                sup[:, 512 * ic + P * sp:512 * ic + P * (sp + 1)],
                                xp_ps[ic % 2]),
                                waits=["a_tp%d_%d" % (t, ic)], inc="SC",
                                ev="a_fme%d_%d" % (t, ic))
                        if tt < 15:
                            a_part1(t + 1)
                        for gg in range(4):
                            for ic in range(8):
                                mw = []
                                if ic == 0:
                                    if gg == 0:
                                        mw = ["a_fme%d_7" % t, "qkvT7",
                                              S.gev("a_kvev%d_2" % (t - 1), "SC")]
                                    elif gg == 1:
                                        mw = [S.gev("a_kvev%d_3" % (t - 1), "SC")]
                                    else:
                                        mw = ["a_ksq%d_%d" % (t, gg - 2)]
                                S.add("pe", lambda pe, sup=sup, sp=sp, ic=ic, gg=gg: pe.matmul(
                                    kvg_ps[gg % 2][:],
                                    sup[:, 512 * ic + P * sp:512 * ic + P * (sp + 1)],
                                    qkvT[:, 3 * D * ic + D + 512 * gg:
                                         3 * D * ic + D + 512 * (gg + 1)],
                                    start=(ic == 0), stop=(ic == 7)),
                                    waits=mw, inc="PE", ev="a_kv%d_%d" % (t, gg))
                            if gg < 2:
                                war = (S.gev("a_qev%d_%d" % (t - 1, 6 + gg), "SC")
                                       if (t - 1) % 4 == 3 else
                                       S.gev("a_kvev%d_%d" % (t - 1, gg), "SC"))
                                S.add("sc", lambda sc, gg=gg: sc.activation(
                                    sq_s[gg][:], kvg_ps[gg % 2][:], AF.Square),
                                    waits=["a_kv%d_%d" % (t, gg), war,
                                           S.gev("a_kvmm%d" % (t - 2), "PE")], inc="SC",
                                    ev="a_ksq%d_%d" % (t, gg))
                                S.add("sc", lambda sc, b=b, gg=gg: sc.activation(
                                    kphi[b][:, 512 * gg:512 * (gg + 1)], sq_s[gg][:],
                                    AF.Exp, scale=-0.5),
                                    waits=["a_ksq%d_%d" % (t, gg)],
                                    inc="SC", ev="a_kvev%d_%d" % (t, gg))
                            else:
                                S.add("sc", lambda sc, b=b, gg=gg: sc.copy(
                                    vext[b][:].rearrange("p (a c) -> p a c", c=129)
                                    [:, 4 * (gg - 2):4 * (gg - 2) + 4, 0:128],
                                    kvg_ps[gg % 2][:].rearrange("p (a c) -> p a c", c=128)),
                                    waits=["a_kv%d_%d" % (t, gg), "vext_ones%d" % b,
                                           S.gev("a_kvmm%d" % (t - 2), "PE")],
                                    inc="SC", ev="a_kvev%d_%d" % (t, gg))
                        for p_ in range(8):
                            kvp, off = kv_slot(p_)
                            mw = []
                            if p_ == 0:
                                mw = ["a_kvev%d_0" % t, "a_kvev%d_1" % t,
                                      "a_kvev%d_2" % t, "a_kvev%d_3" % t]
                                if tt == 0 and seq == 1:
                                    mw.append("b0_kvsb2")
                            S.add("pe", lambda pe, b=b, p_=p_, kvp=kvp, off=off, tt=tt: pe.matmul(
                                kvp[:, off:off + 129],
                                kphi[b][:, P * p_:P * (p_ + 1)],
                                vext[b][:, 129 * p_:129 * (p_ + 1)],
                                start=False, stop=(tt == 15 and p_ in (2, 5, 7))),
                                waits=mw, inc="PE", ev=("a_kvmm%d" % t) if p_ == 7 else None)
                        if sp == 3:
                            for qc in range(8):
                                for ic in range(8):
                                    mw = []
                                    if ic == 0 and qc >= 1:
                                        mw = [S.gev("a_qsq%d_%d" % (t, qc - 1), "SC")]
                                    elif ic == 0 and qc == 0:
                                        mw = [S.gev("a_qsq%d_7" % (t - 4), "SC")]
                                    S.add("pe", lambda pe, sup=sup, qc=qc, ic=ic: pe.matmul(
                                        q_ps[0][:],
                                        qkvT[:, 3 * D * ic + P * qc:3 * D * ic + P * (qc + 1)],
                                        sup[:, 512 * ic:512 * (ic + 1)],
                                        start=(ic == 0), stop=(ic == 7)),
                                        waits=mw, inc="PE", ev="a_q%d_%d" % (t, qc))
                                qwar = (["a_kvev%d_%d" % (t, qc)] if qc < 2 else
                                        [S.gev("a_qev%d_%d" % (t, qc - 2), "SC")])
                                S.add("sc", lambda sc, qc=qc: sc.activation(
                                    sq_s[qc % 2][:], q_ps[0][:], AF.Square),
                                    waits=["a_q%d_%d" % (t, qc)] + qwar, inc="SC",
                                    ev="a_qsq%d_%d" % (t, qc))
                                S.add("sc", lambda sc, qc=qc, tt=tt: sc.activation(
                                    phi_q[:, 2048 * qc + 512 * (tt // 4):
                                          2048 * qc + 512 * (tt // 4) + 512],
                                    sq_s[qc % 2][:], AF.Exp, scale=-0.5),
                                    waits=["a_qsq%d_%d" % (t, qc)],
                                    inc="SC", ev="a_qev%d_%d" % (t, qc))
                    S.mark("pA%d" % seq)
                    tL = t0 + 15
                    for i, (p_lo, w_) in enumerate([(0, 387), (3, 387), (6, 258)]):
                        kvp = kv_slot(p_lo)[0][:, 0:387 if p_lo < 6 else 258]
                        S.add("v", lambda v, kvp=kvp, w_=w_, i=i: v.tensor_copy(
                            kv_sb[:, 387 * i:387 * i + w_], kvp),
                            waits=["a_kvmm%d" % tL, "a_qev%d_7" % tL],
                            inc="V", ev="b%d_kvsb%d" % (seq, i))
                    for p_ in range(8):
                        kvp, off = kv_slot(p_)
                        S.add("v", lambda v, kvp=kvp, off=off, p_=p_: v.tensor_copy(
                            kv_bd[0:64, P * p_:P * p_ + 64], kvp[0:64, off:off + 64]),
                            waits=(["b%d_kvsb2" % seq, "kvbd_zero"] +
                                   ([S.gev("c_at%d" % 15, "PE")] if seq == 1 else []))
                                  if p_ == 0 else [], inc="V")
                        S.add("v", lambda v, kvp=kvp, off=off, p_=p_: v.tensor_copy(
                            kv_bd[64:128, P * p_ + 64:P * (p_ + 1)], kvp[64:128, off:off + 64]),
                            inc="V", ev="b%d_kvbd%d" % (seq, p_))
                    bw = ["b%d_kvsb2" % seq]
                    if seq == 1:
                        bw.append(S.gev("b0_zmm7_3", "PE"))
                    S.add("v", lambda v: v.memset(bd[:], 0.0), waits=bw,
                          inc="V", ev="b%d_bd0" % seq)
                    for p_ in range(8):
                        kvp, off = kv_slot(p_)
                        S.add("v", lambda v, kvp=kvp, off=off, p_=p_: v.tensor_copy(
                            bd[0:64, 2 * p_:2 * p_ + 1], kvp[0:64, off + 128:off + 129]),
                            waits=["b%d_bd0" % seq] if p_ == 0 else [], inc="V")
                        S.add("v", lambda v, kvp=kvp, off=off, p_=p_: v.tensor_copy(
                            bd[64:128, 2 * p_ + 1:2 * p_ + 2], kvp[64:128, off + 128:off + 129]),
                            inc="V", ev="b%d_bd%d" % (seq, p_ + 1))
                    if True:
                        zden_ps = kvCz[:, 400:416]
                        for tt in range(16):
                            for p_ in range(8):
                                mw = ["b%d_bd8" % seq] if (tt == 0 and p_ == 0) else \
                                     (["b%d_zev%d" % (seq, tt - 1)] if p_ == 0 else [])
                                S.add("pe", lambda pe, p_=p_, tt=tt, zden_ps=zden_ps: pe.matmul(
                                    zden_ps[:, 2 * p_:2 * p_ + 2],
                                    phi_q[:, 2048 * p_ + P * tt:2048 * p_ + P * (tt + 1)],
                                    bd[:, 2 * p_:2 * p_ + 2],
                                    start=True, stop=True),
                                    waits=mw, inc="PE", ev="b%d_zmm%d" % (seq, tt))
                            S.add("v", lambda v, tt=tt, zden_ps=zden_ps: v.tensor_copy(
                                zq_sb[:, 16 * tt:16 * (tt + 1)], zden_ps),
                                waits=["b%d_zmm%d" % (seq, tt)],
                                inc="V", ev="b%d_zev%d" % (seq, tt))
                    zw = ["b%d_zev15" % seq]
                    if seq == 1:
                        zw.append(S.gev("c_atz%d_1" % 15, "V"))
                    S.add("v", lambda v: v.tensor_scalar(zn1[:], zq_sb[:], 1.0, 1e-6,
                                                         op.mult, op.add), waits=zw, inc="V")
                    v_drain()
                    recip_nr(zn1[:], z_tm[:], zni[:], zn2[:], zq_sb[:], ev="b%d_z" % seq)
                S.mark("pB%d" % seq)
                with ExitStack() as sC:
                    eC = sC.enter_context
                    atps = [eC(nc.psum_tensor("at%d_%d" % (seq, i), [P, 512], F32))
                            for i in range(4)]
                    prA = eC(nc.psum_tensor("prA%d" % seq, [P, 512], F32))
                    prB = eC(nc.psum_tensor("prB%d" % seq, [P, 512], F32))
                    def c_store(t):
                        S.add("sy", lambda sy, t=t: sy.dma_start(
                            out=g["x2_h"][P * t:P * (t + 1), :], in_=x2_t[t % 2][:]),
                            waits=["c_x2%d" % t], inc="SX%d" % (t % 2), ev="c_x2st%d" % t)

                    for tt in range(16):
                        t = t0 + tt
                        b = t % 2
                        ldw = [S.gev("c_xn%d" % (t - 2), "SC")]
                        if tt < 2:
                            ldw.append("a_xn%d" % (t0 + 14 + tt))
                        S.add("sy", lambda sy, t=t, b=b: sy.dma_start(
                            out=xa[b][:], in_=g["x"][P * t:P * (t + 1), :]),
                            waits=ldw, inc="XA%d" % b, ev="c_x%d" % t)
                        if tt >= 1:
                            c_store(t - 1)
                        xnw = ["c_x%d" % t, S.gev("c_x2%d" % (t - 2), "V")]
                        if tt < 2:
                            xnw.append(S.gev("a_xn1%d" % (t0 + 14 + tt), "V"))
                        S.add("sc", lambda sc, t=t, b=b: sc.activation(
                            xn_t[b][:], xa[b][:], AF.Identity,
                            bias=mrs_all[:, t:t + 1], scale=rstd_all[:, t:t + 1]),
                            waits=xnw, inc="SC", ev="c_xn%d" % t)
                        for p_ in range(8):
                            ps_h = atps[2 * (t % 2) + (0 if p_ < 4 else 1)]
                            po = (p_ % 4) * P
                            mw = []
                            if p_ == 0:
                                mw = ["b%d_kvbd7" % seq, S.gev("c_atz%d_1" % (t - 2), "V")]
                            S.add("pe", lambda pe, p_=p_, tt=tt, ps_h=ps_h, po=po: pe.matmul(
                                ps_h[:, po:po + P],
                                phi_q[:, 2048 * p_ + P * tt:2048 * p_ + P * (tt + 1)],
                                kv_bd[:, P * p_:P * (p_ + 1)],
                                start=True, stop=True),
                                waits=mw, inc="PE",
                                ev=("c_at%d" % t) if p_ == 7 else None)
                        for hh in range(2):
                            ps_h = atps[2 * (t % 2) + hh]
                            S.add("v", lambda v, hh=hh, ps_h=ps_h: v.tensor_tensor(
                                atz[:, 512 * hh:512 * (hh + 1)]
                                .rearrange("p (h e) -> p h e", e=64),
                                ps_h[:].rearrange("p (h e) -> p h e", e=64),
                                z_tm[:, 16 * tt + 8 * hh:16 * tt + 8 * (hh + 1)]
                                .rearrange("p h -> p h ()").broadcast_to([P, 8, 64]),
                                op.mult),
                                waits=["c_at%d" % t, "b%d_z" % seq] +
                                      ([S.gev("c_atp%d_7" % (t - 1), "PE")] if hh == 0 else []),
                                inc="V", ev="c_atz%d_%d" % (t, hh))
                        for ic in range(8):
                            tw = ["c_atz%d_1" % t]
                            if ic < 2:
                                tw.append(S.gev("c_afe%d_%d" % (t - 1, ic + 6), "SC"))
                            else:
                                tw.append("c_afe%d_%d" % (t, ic - 2))
                            S.add("pe", lambda pe, ic=ic: pe.transpose(
                                xp_ps[ic % 2], atz[:, P * ic:P * (ic + 1)], ident_b[:]),
                                waits=tw, inc="PE", ev="c_atp%d_%d" % (t, ic))
                            S.add("sc", lambda sc, ic=ic: sc.copy(
                                atf[:, P * ic:P * (ic + 1)], xp_ps[ic % 2]),
                                waits=["c_atp%d_%d" % (t, ic),
                                       S.gev("c_proj%d" % (t - 1), "PE") if ic == 0 else None],
                                inc="SC", ev="c_afe%d_%d" % (t, ic))
                        for hh, ps_h in enumerate((prA, prB)):
                            for ic in range(8):
                                mw = []
                                if ic == 0:
                                    mw = ["c_afe%d_7" % t] + \
                                         (["projT7"] if t == 0 else []) + \
                                         [S.gev("c_x2%d" % (t - 1), "V")]
                                S.add("pe", lambda pe, ic=ic, hh=hh, ps_h=ps_h: pe.matmul(
                                    ps_h[:], atf[:, P * ic:P * (ic + 1)],
                                    projT[:, D * ic + 512 * hh:D * ic + 512 * (hh + 1)],
                                    start=(ic == 0), stop=(ic == 7)),
                                    waits=mw, inc="PE",
                                    ev=("c_proj%d" % t) if (ic == 7 and hh == 1) else None)
                        for hh, ps_h in enumerate((prA, prB)):
                            sl = slice(512 * hh, 512 * (hh + 1))
                            S.add("v", lambda v, ps_h=ps_h, sl=sl: v.tensor_tensor(
                                x2w[:, sl], ps_h[:], reps["pb"][:, sl], op.add),
                                waits=["c_proj%d" % t] +
                                      (([S.gev("c_x2sq%d" % (t - 1), "SC"),
                                         S.gev("c_ssq%d" % (t - 1), "V")] +
                                        (["c_x2st%d" % (t - 2)] if t >= 2 else []))
                                       if hh == 0 else []),
                                inc="V", ev="c_xq%d_%d" % (t, hh))
                        for hh in range(2):
                            sl = slice(512 * hh, 512 * (hh + 1))
                            S.add("v", lambda v, sl=sl: v.tensor_tensor(
                                x2w[:, sl], x2w[:, sl], reps["gm"][:, sl], op.mult),
                                waits=["c_xq%d_%d" % (t, hh)], inc="V",
                                ev="c_xq%d_%d" % (t, 2 + hh))
                        for hh in range(2):
                            sl = slice(512 * hh, 512 * (hh + 1))
                            S.add("v", lambda v, b=b, sl=sl: v.tensor_tensor(
                                x2_t[b][:, sl], x2w[:, sl], xn_t[b][:, sl], op.add),
                                waits=["c_xq%d_%d" % (t, 2 + hh)] +
                                      (["c_xn%d" % t] if hh == 0 else []), inc="V",
                                ev=("c_x2%d" % t) if hh == 1 else None)
                        S.add("sc", lambda sc, b=b: sc.activation(x2w[:], x2_t[b][:], AF.Square),
                              waits=["c_x2%d" % t, S.gev("c_ssq%d" % (t - 1), "V")],
                              inc="SC", ev="c_x2sq%d" % t)
                        S.add("v", lambda v, t=t: v.reduce_sum(ssq2_all[:, t:t + 1], x2w[:], AX),
                              waits=["c_x2sq%d" % t], inc="V", ev="c_ssq%d" % t)
                    c_store(t0 + 15)

                S.mark("pC%d" % seq)
        def make_quant_tiles(e_, tag, W):
            q = {"W": W}
            for nm in ("u", "u2", "sgn", "s5", "s6", "tA", "tB"):
                q[nm] = e_(nc.sbuf_tensor("q%s_%s" % (tag, nm), [P, W], F32))
            q["s7"] = q["u"]
            nb = W // 16
            for nm in ("am", "sca", "w1", "w2"):
                q[nm] = e_(nc.sbuf_tensor("q%s_%s" % (tag, nm), [P, nb], F32))
            q["rec"] = e_(nc.sbuf_tensor("q%s_rec" % tag, [P, nb], F32))
            q["wi"] = e_(nc.sbuf_tensor("q%s_wi" % tag, [P, nb], I32))
            return q

        _qid = [0]

        def quant_gen(q, src_ap, dst_ap, W, waits_src, done_ev):
            nb = W // 16
            uid = _qid[0]
            _qid[0] += 1
            e = lambda s_: "qz%d_%s" % (uid, s_)
            x3 = src_ap.rearrange("p (nb b) -> p nb b", b=16)
            am, sca, w1, w2, rec, wi = (q[k][:, :nb] for k in ("am", "sca", "w1", "w2", "rec", "wi"))
            u, u2, sgn, s5, s6, s7, tA, tB = (q[k][:, :W] for k in
                                              ("u", "u2", "sgn", "s5", "s6", "s7", "tA", "tB"))
            S.add("v", lambda v: v.tensor_reduce(am, x3, AX, op.max, apply_absolute_value=True),
                  waits=list(waits_src), inc="V", ev=e("am"))
            yield
            S.add("v", lambda v: v.tensor_scalar(sca, am, 1.0 / 6.0, 1e-12, op.mult, op.max),
                  waits=[e("am")], inc="V", ev=e("sca"))
            yield
            S.add("v", lambda v: v.tensor_scalar(wi, sca.bitcast(I32), 16, 0x4B000000,
                                                 op.logical_shift_right, op.bitwise_or),
                  waits=[e("sca")], inc="V", ev=e("wi1"))
            yield
            S.add("v", lambda v: v.tensor_scalar(w1, wi.bitcast(F32), -1.0,
                                                 float(0x1000000 + 0x7EF1), op.mult, op.add),
                  waits=[e("wi1")], inc="V", ev=e("w1a"))
            yield
            S.add("v", lambda v: v.tensor_scalar(wi, w1.bitcast(I32), 0x7FFFFF, 16,
                                                 op.bitwise_and, op.logical_shift_left),
                  waits=[e("w1a")], inc="V", ev=e("wi2"))
            yield
            yv = wi.bitcast(F32)
            prev = e("wi2")
            NRI = 3
            for it in range(NRI):
                S.add("v", lambda v, yv=yv, prev=prev: v.tensor_tensor(w2, sca, yv, op.mult),
                      waits=[prev], inc="V", ev=e("w2a%d" % it))
                yield
                S.add("v", lambda v, it=it: v.tensor_scalar(w2, w2, -1.0, 2.0, op.mult, op.add),
                      waits=[e("w2a%d" % it)], inc="V", ev=e("w2b%d" % it))
                yield
                dst = rec if it == NRI - 1 else w1
                S.add("v", lambda v, yv=yv, dst=dst, it=it: v.tensor_tensor(dst, yv, w2, op.mult),
                      waits=[e("w2b%d" % it)], inc="V", ev=e("nr%d" % it))
                yv = w1
                prev = e("nr%d" % it)
                yield
            rb = rec.rearrange("p nb -> p nb ()").broadcast_to([P, nb, 16])
            S.add("v", lambda v: v.tensor_tensor(
                u.rearrange("p (nb b) -> p nb b", b=16), x3, rb, op.mult),
                waits=[e("nr%d" % (NRI - 1))], inc="V", ev=e("u"))
            yield
            S.add("v", lambda v: v.tensor_tensor(u2, u, u, op.mult),
                  waits=[e("u")], inc="V", ev=e("u2"))
            yield
            S.add("sc", lambda sc: sc.activation(sgn, u, AF.Sign),
                  waits=[e("u")], inc="SC", ev=e("sgn"))
            yield
            S.add("sc", lambda sc: sc.activation(s5, u2, AF.Sign, bias=thrB[:, 0:1]),
                  waits=[e("u2"), "thrB"], inc="SC", ev=e("s5"))
            yield
            S.add("sc", lambda sc: sc.activation(s6, u2, AF.Sign, bias=thrB[:, 1:2]),
                  inc="SC", ev=e("s6"))
            yield
            S.add("sc", lambda sc: sc.activation(s7, u2, AF.Sign, bias=thrB[:, 2:3]),
                  waits=[e("sgn")], inc="SC", ev=e("s7"))
            yield
            S.add("v", lambda v: v.tensor_tensor(s5, s5, s6, op.add),
                  waits=[e("s5"), e("s6")], inc="V", ev=e("sadd"))
            yield
            S.add("v", lambda v: v.scalar_tensor_tensor(s6, s7, 2.0, s5, op.mult, op.add),
                  waits=[e("s7"), e("sadd")], inc="V", ev=e("smul"))
            yield
            S.add("v", lambda v: v.scalar_tensor_tensor(tA, u2, 0.0625, s6, op.is_gt, op.add),
                  waits=[e("u2"), e("smul")], inc="V", ev=e("tA1"))
            yield
            S.add("v", lambda v: v.scalar_tensor_tensor(tB, u2, 0.5625, tA, op.is_gt, op.add),
                  waits=[e("tA1")], inc="V", ev=e("tB1"))
            yield
            S.add("v", lambda v: v.scalar_tensor_tensor(tA, u2, 1.5625, tB, op.is_gt, op.add),
                  waits=[e("tB1")], inc="V", ev=e("tA2"))
            yield
            S.add("v", lambda v: v.scalar_tensor_tensor(tB, u2, 3.0625, tA, op.is_gt, op.add),
                  waits=[e("tA2")], inc="V", ev=e("tB2"))
            yield
            sb_ = sca.rearrange("p nb -> p nb ()").broadcast_to([P, nb, 16])
            S.add("v", lambda v: v.scalar_tensor_tensor(
                tA.rearrange("p (nb b) -> p nb b", b=16),
                sgn.rearrange("p (nb b) -> p nb b", b=16), 0.5, sb_, op.mult, op.mult),
                waits=[e("sgn"), e("tB2")], inc="V", ev=e("tBs"))
            yield
            S.add("v", lambda v: v.scalar_tensor_tensor(dst_ap, tB, 4.0, tA, op.add, op.mult),
                  waits=[e("tBs")], inc="V", ev=done_ev)

        def run_gens(gens):
            gens = list(gens)
            while gens:
                nxt = []
                for gn in gens:
                    try:
                        next(gn)
                        nxt.append(gn)
                    except StopIteration:
                        continue
                    else:
                        pass
                gens = nxt



        def wq_chunk_gen(ui, mat, dstm, rc, width, uw, qts, win_t, wq_t, extra_waits):
            slot = ui % 2
            S.add("sy", lambda sy: sy.dma_start(
                out=win_t[:, 0:width], in_=g[mat][P * rc:P * (rc + 1), :]),
                waits=[S.gev("wq_done%d" % (ui - 2), "V")] +
                      (["wq_st%d" % (ui - 2)] if ui >= 2 else []) + list(extra_waits),
                inc="WI%d" % slot, ev="wq_ld%d" % ui)
            yield
            nun = width // uw
            if nun == 2:
                gens = [quant_gen(qts[un], win_t[:, uw * un:uw * (un + 1)],
                                  wq_t[:, uw * un:uw * (un + 1)], uw,
                                  ["wq_ld%d" % ui] +
                                  (["wq_st%d" % (ui - 1)] if ui >= 1 else []) +
                                  [S.gev("wq_q%d_%d" % (ui - 1, un), "V")],
                                  "wq_q%d_%d" % (ui, un))
                        for un in range(2)]
                gens = [g_ for g_ in gens]
                while gens:
                    nxt = []
                    for gn in gens:
                        try:
                            next(gn)
                            nxt.append(gn)
                        except StopIteration:
                            continue
                    gens = nxt
                    yield
                last = "wq_q%d_1" % ui
            else:
                for half in range(2):
                    gens = [quant_gen(qts[un % 2], win_t[:, uw * un:uw * (un + 1)],
                                      wq_t[:, uw * un:uw * (un + 1)], uw,
                                      ["wq_ld%d" % ui] +
                                      (["wq_st%d" % (ui - 2)] if ui >= 2 else []) +
                                      ([S.gev("wq_q%d_%d" % (ui - 1, un % 2 + 2), "V")]
                                       if un < 2 else ["wq_q%d_%d" % (ui, un - 2)]),
                                      "wq_q%d_%d" % (ui, un))
                            for un in (2 * half, 2 * half + 1)]
                    gens = [g_ for g_ in gens]
                    while gens:
                        nxt = []
                        for gn in gens:
                            try:
                                next(gn)
                                nxt.append(gn)
                            except StopIteration:
                                continue
                        gens = nxt
                        yield
                last = "wq_q%d_3" % ui
            S.ev["wq_done%d" % ui] = S.cnt["V"]
            S.evsem["wq_done%d" % ui] = "V"
            S.add("sy", lambda sy: sy.dma_start(
                out=g[dstm][P * rc:P * (rc + 1), :], in_=wq_t[:, 0:width]),
                waits=[last], inc="SW%d" % slot, ev="wq_st%d" % ui)
            yield

        def wq_pump_all(chunks):
            gens = [wq_chunk_gen(*args) for args in chunks]
            prev = None
            for gn in gens:
                try:
                    next(gn)
                except StopIteration:
                    gn = None
                yield
                if prev is not None:
                    for _ in prev:
                        yield
                prev = gn
            if prev is not None:
                for _ in prev:
                    yield

        S.mark("p1")
        with ExitStack() as p2:
            e2 = p2.enter_context
            conv1T = e2(nc.sbuf_tensor("conv1T", [P, 8 * HID], BF16))
            conv2T = e2(nc.sbuf_tensor("conv2T", [P, 16 * D], BF16))
            x2b = [e2(nc.sbuf_tensor("x2b%d" % i, [P, D], BF16)) for i in range(4)]
            x2f = [e2(nc.sbuf_tensor("x2f%d" % i, [P, D], F32)) for i in range(4)]
            xn2_0 = e2(nc.sbuf_tensor("xn2_0", [P, D], BF16))
            xn2 = [xn2_0, xn2_0]
            fm20 = e2(nc.sbuf_tensor("fm2_0", [P, 8 * 512], BF16))
            fm2 = [fm20, fm20]
            hddf0 = e2(nc.sbuf_tensor("hddf0", [P, 16 * 512], BF16))
            hddf = [hddf0, hddf0]
            qtw = [make_quant_tiles(e2, "gw%d" % i, 512) for i in range(2)]
            winw = [e2(nc.sbuf_tensor("winw%d" % i, [P, D], F32)) for i in range(2)]
            wqw0 = e2(nc.sbuf_tensor("wqw0", [P, D], BF16))
            wqw = [wqw0, wqw0]
            gu_chunks = []
            for _ui, (_mat, _dstm) in enumerate(
                    [("L0_ffn_w_gate", "wqg_h")] * 22 + [("L0_ffn_w_up", "wqu_h")] * 22):
                _rc = _ui % 22
                gu_chunks.append((_ui, _mat, _dstm, _rc, D, 512,
                                  qtw, winw[_ui % 2], wqw[_ui % 2], []))
            wq_pump = wq_pump_all(gu_chunks)

            def pump(n):
                for _ in range(n):
                    if next(wq_pump, "END") == "END":
                        break
            s_t0 = e2(nc.sbuf_tensor("s_t0", [P, D], BF16))
            s_t = [s_t0, s_t0]
            scc = e2(nc.sbuf_tensor("scc", [P, 2 * 512], F32))
            c1_ps = [e2(nc.psum_tensor("c1ps%d" % i, [P, 512], F32)) for i in range(2)]
            c2_ps4 = [e2(nc.psum_tensor("c2ps%d" % i, [P, 512], F32)) for i in range(4)]
            xp2_t = [e2(nc.psum_tensor("xp2t%d" % i, [P, P], BF16)) for i in range(2)]
            xp2_ps = [xp2_t[0][:], xp2_t[1][:]]

            for ic in range(8):
                S.add("sy", lambda sy, ic=ic: sy.dma_start_transpose(
                    out=conv1T[:, HID * ic:HID * (ic + 1)],
                    in_=g["c1_bf"][:, P * ic:P * (ic + 1)]),
                    waits=["cast_c1"] + (["c_ssq31", "c_proj31", "c_x2sq31",
                                          "c_x2st31", "c_x2st30"] if ic == 0 else []),
                    inc="T1", ev="c1T%d" % ic)
            for ic in range(16):
                S.add("sy", lambda sy, ic=ic: sy.dma_start_transpose(
                    out=conv2T[:, D * ic:D * (ic + 1)],
                    in_=g["c2_bf"][:, P * ic:P * (ic + 1)]),
                    waits=["cast_c2"], inc="T2", ev="c2T%d" % ic)
            S.add("v", lambda v: v.drain(), waits=["c_ssq%d" % (NT - 1)])
            rsqrt_nr(ssq2_all[:], rstd2_all[:], sti[:], st1[:], st2[:], st3[:])
            v_drain()
            S.add("v", lambda v: v.tensor_scalar(rstd2_all[:], rstd2_all[:], C32, None, op.mult),
                  inc="V", ev="rstd2_done")

            for sp in range(8):
                sup = fm2[sp % 2]
                hdd = hddf[sp % 2]
                for tt in range(4):
                    t = 4 * sp + tt
                    b4 = t % 4
                    b2 = t % 2
                    S.add("sy", lambda sy, t=t, b4=b4: sy.dma_start(
                        out=x2b[b4][:], in_=g["x2_h"][P * t:P * (t + 1), :]),
                        waits=["c_x2st%d" % t, S.gev("p2_s%d" % (t - 4), "V")],
                        inc="XB%d" % b4, ev="p2_x2%d" % t)
                    S.add("sc", lambda sc, b4=b4: sc.copy(x2f[b4][:], x2b[b4][:]),
                          waits=["p2_x2%d" % t, S.gev("p2_s%d" % (t - 4), "V")],
                          inc="SC", ev="p2_x2f%d" % t)
                    S.add("v", lambda v, t=t, b4=b4, b2=b2: v.scalar_tensor_tensor(
                        xn2[b2][:], x2f[b4][:], rstd2_all[:, t:t + 1], reps["s2"][:],
                        op.mult, op.mult),
                        waits=["p2_x2f%d" % t, "rstd2_done",
                               S.gev("p2_fme%d_7" % (t - 1), "SC")],
                        inc="V", ev="p2_xn2%d" % t)
                    for ic in range(8):
                        tw = ["p2_xn2%d" % t]
                        if ic < 2:
                            tw.append(S.gev("p2_fme%d_%d" % (t - 1, ic + 6), "SC"))
                        else:
                            tw.append("p2_fme%d_%d" % (t, ic - 2))
                        S.add("pe", lambda pe, b2=b2, ic=ic: pe.transpose(
                            xp2_ps[ic % 2], xn2[b2][:, P * ic:P * (ic + 1)], ident_b[:]),
                            waits=tw, inc="PE", ev="p2_tp%d_%d" % (t, ic))
                        fw = ["p2_tp%d_%d" % (t, ic)]
                        if tt == 0 and ic == 0:
                            fw.append(S.gev("p2_c1%d_15" % (sp - 1), "PE"))
                        S.add("sc", lambda sc, sup=sup, tt=tt, ic=ic: sc.copy(
                            sup[:, 512 * ic + P * tt:512 * ic + P * (tt + 1)],
                            xp2_ps[ic % 2]),
                            waits=fw, inc="SC", ev="p2_fme%d_%d" % (t, ic))
                    pump(55)
                tl = 4 * sp + 3
                for oc in range(16):
                    for ic in range(8):
                        mw = []
                        if ic == 0:
                            mw = ["p2_fme%d_7" % tl] + (["c1T7"] if sp == 0 else []) + \
                                 ([("SC", S.ev.get("p2_gel%d_%d" % (sp, oc - 2), 0))]
                                  if oc >= 2 else
                                  [S.gev("p2_gel%d_%d" % (sp - 1, oc + 14), "SC")])
                        S.add("pe", lambda pe, sup=sup, oc=oc, ic=ic: pe.matmul(
                            c1_ps[oc % 2][:],
                            conv1T[:, HID * ic + P * oc:HID * ic + P * (oc + 1)],
                            sup[:, 512 * ic:512 * (ic + 1)],
                            start=(ic == 0), stop=(ic == 7)),
                            waits=mw, inc="PE", ev="p2_c1%d_%d" % (sp, oc))
                    gw = ["p2_c1%d_%d" % (sp, oc)]
                    if oc == 0:
                        gw.append(S.gev("p2_c2%d" % (4 * (sp - 1) + 3), "PE"))
                    S.add("sc", lambda sc, hdd=hdd, oc=oc: sc.activation(
                        hdd[:, 512 * oc:512 * (oc + 1)], c1_ps[oc % 2][:], AF.Gelu,
                        bias=c1b_fm[:, oc:oc + 1]),
                        waits=gw, inc="SC", ev="p2_gel%d_%d" % (sp, oc))
                for tt in range(4):
                    t = 4 * sp + tt
                    b4 = t % 4
                    b2 = t % 2
                    if t >= 1:
                        S.add("sy", lambda sy, t=t: sy.dma_start(
                            out=g["s_h"][P * (t - 1):P * t, :], in_=s_t[(t - 1) % 2][:]),
                            waits=["p2_s%d" % (t - 1)], inc="SS%d" % ((t - 1) % 2),
                            ev="p2_sst%d" % (t - 1))
                    for hh in range(2):
                        for ic in range(16):
                            mw = []
                            if ic == 0:
                                mw = ["p2_gel%d_15" % sp, S.gev("p2_s%d" % (t - 2), "V")] + \
                                     (["c2T15"] if t == 0 else [])
                            S.add("pe", lambda pe, hdd=hdd, t=t, tt=tt, ic=ic, hh=hh: pe.matmul(
                                c2_ps4[2 * (t % 2) + hh][:],
                                hdd[:, 512 * ic + P * tt:512 * ic + P * (tt + 1)],
                                conv2T[:, D * ic + 512 * hh:D * ic + 512 * (hh + 1)],
                                start=(ic == 0), stop=(ic == 15)),
                                waits=mw, inc="PE",
                                ev=("p2_c2%d" % t) if (ic == 15 and hh == 1) else None)
                    for hh in range(2):
                        sl = slice(512 * hh, 512 * (hh + 1))
                        S.add("v", lambda v, t=t, hh=hh, sl=sl: v.tensor_tensor(
                            scc[:, sl], c2_ps4[2 * (t % 2) + hh][:], reps["c2b"][:, sl],
                            op.add),
                            waits=(["p2_c2%d" % t] +
                                   (["p2_sst%d" % (t - 1)] if t >= 1 else []) +
                                   ([S.gev("p2_sq%d_4" % (t - 1), "V")] if t >= 1 else [])
                                   if hh == 0 else
                                   [S.gev("p2_s%d" % (t - 1), "V")]),
                            inc="V", ev="p2_sq%d_%d" % (t, hh))
                    for hh in range(2):
                        sl = slice(512 * hh, 512 * (hh + 1))
                        S.add("v", lambda v, sl=sl: v.tensor_tensor(
                            scc[:, sl], scc[:, sl], reps["gm"][:, sl], op.mult),
                            waits=["p2_sq%d_%d" % (t, hh)], inc="V",
                            ev="p2_sq%d_%d" % (t, 2 + hh))
                    for hh in range(2):
                        sl = slice(512 * hh, 512 * (hh + 1))
                        S.add("v", lambda v, b4=b4, b2=b2, sl=sl, hh=hh: v.tensor_tensor(
                            s_t[b2][:, sl], scc[:, sl], x2f[b4][:, sl], op.add),
                            waits=["p2_sq%d_%d" % (t, 2 + hh)] +
                                  (["p2_sst%d" % (t - 2)] if (hh == 0 and t >= 2) else []),
                            inc="V", ev=("p2_s%d" % t) if hh == 1 else "p2_sq%d_4" % t)
                    pump(25)
            S.add("sy", lambda sy: sy.dma_start(
                out=g["s_h"][P * 31:P * 32, :], in_=s_t[31 % 2][:]),
                waits=["p2_s31"], inc="SS%d" % (31 % 2), ev="p2_sst31")
            pump(10 ** 9)

        S.mark("p2")
        midstack.close()

        with ExitStack() as p25:
            e25 = p25.enter_context
            qt = [make_quant_tiles(e25, "w%d" % i, 704) for i in range(2)]
            win = [e25(nc.sbuf_tensor("win%d" % i, [P, IFF], F32)) for i in range(2)]
            wq = [e25(nc.sbuf_tensor("wq%d" % i, [P, IFF], BF16)) for i in range(2)]
            dn_chunks = []
            for _j in range(8):
                _ui = 44 + _j
                extra = (["p2_s31", "p2_c231", "p2_gel7_15", "p2_sst31", "p2_sst30",
                          "wq_st43", "wq_st42", "wq_done43"] if _j == 0 else [])
                dn_chunks.append((_ui, "L0_ffn_w_down", "wqd_h", _j, IFF, 704,
                                  qt, win[_ui % 2], wq[_ui % 2], extra))
            for _ in wq_pump_all(dn_chunks):
                pass

        S.mark("p25")
        CGS = [(0, 512), (512, 512), (1024, 512), (1536, 512), (2048, 512), (2560, 256)]
        with ExitStack() as pa1:
            ea = pa1.enter_context
            wqgT = ea(nc.sbuf_tensor("wqgT", [P, 8 * IFF], BF16))
            qta = [make_quant_tiles(ea, "a%d" % i, 512) for i in range(2)]
            xq = [ea(nc.sbuf_tensor("xq%d" % i, [P, D], BF16)) for i in range(2)]
            xqf = [ea(nc.sbuf_tensor("xqf%d" % i, [P, D], BF16)) for i in range(2)]
            gate_t = [ea(nc.sbuf_tensor("gate%d" % i, [P, IFF], F32)) for i in range(2)]
            g_ps = [ea(nc.psum_tensor("gps%d" % i, [P, 512], F32)) for i in range(4)]
            xp3_t = [ea(nc.psum_tensor("xp3t%d" % i, [P, P], BF16)) for i in range(2)]
            xp3_ps = [xp3_t[0][:], xp3_t[1][:]]
            for ic in range(8):
                S.add("sy", lambda sy, ic=ic: sy.dma_start_transpose(
                    out=wqgT[:, IFF * ic:IFF * (ic + 1)],
                    in_=g["wqg_h"][:, P * ic:P * (ic + 1)]),
                    waits=["wq_st21", "wq_st20"] +
                          (["wq_st51", "wq_st50", "wq_done51"] if ic == 0 else []),
                    inc="TG", ev="wqgT%d" % ic)
            def a1_stores(t):
                S.add("sy", lambda sy, t=t: sy.dma_start(
                    out=g["xqT_h"][P * t:P * (t + 1), :], in_=xqf[t % 2][:]),
                    waits=["a1_fme%d_7" % t], inc="SQ%d" % (t % 2), ev="a1_xqst%d" % t)
                S.add("sy", lambda sy, t=t: sy.dma_start(
                    out=g["gate_h"][P * t:P * (t + 1), :], in_=gate_t[t % 2][:]),
                    waits=["a1_sil%d_5" % t], inc="SG%d" % (t % 2), ev="a1_gst%d" % t)

            for t in range(NT):
                b = t % 2
                ldw = [S.gev("a1_xn%d" % (t - 2), "SC")]
                if t < 2:
                    ldw.append("c_xn%d" % (30 + t))
                S.add("sy", lambda sy, t=t, b=b: sy.dma_start(
                    out=xa[b][:], in_=g["x"][P * t:P * (t + 1), :]),
                    waits=ldw, inc="XA%d" % b, ev="a1_x%d" % t)
                if t >= 1:
                    a1_stores(t - 1)
                xnw = ["a1_x%d" % t, S.gev("a1_q%d_1" % (t - 2), "V")]
                S.add("v", lambda v, t=t, b=b: v.tensor_scalar(
                    xn_t[b][:], xa[b][:], rstd_all[:, t:t + 1], mrs_all[:, t:t + 1],
                    op.mult, op.add),
                    waits=xnw, inc="V", ev="a1_xn%d" % t)
                gens = [quant_gen(qta[un], xn_t[b][:, 512 * un:512 * (un + 1)],
                                  xq[b][:, 512 * un:512 * (un + 1)], 512,
                                  ["a1_xn%d" % t,
                                   S.gev("a1_fme%d_7" % (t - 2), "SC")],
                                  "a1_q%d_%d" % (t, un))
                        for un in range(2)]
                run_gens(gens)
                for ic in range(8):
                    tw = ["a1_q%d_0" % t, "a1_q%d_1" % t]
                    if ic < 2:
                        tw.append(S.gev("a1_fme%d_%d" % (t - 1, ic + 6), "SC"))
                    else:
                        tw.append("a1_fme%d_%d" % (t, ic - 2))
                    S.add("pe", lambda pe, b=b, ic=ic: pe.transpose(
                        xp3_ps[ic % 2], xq[b][:, P * ic:P * (ic + 1)], ident_b[:]),
                        waits=tw, inc="PE", ev="a1_tp%d_%d" % (t, ic))
                    fw = ["a1_tp%d_%d" % (t, ic)]
                    if ic == 0:
                        fw += [S.gev("a1_g%d_5" % (t - 2), "PE")] + \
                              (["a1_xqst%d" % (t - 2)] if t >= 2 else [])
                    S.add("sc", lambda sc, b=b, ic=ic: sc.copy(
                        xqf[b][:, P * ic:P * (ic + 1)], xp3_ps[ic % 2]),
                        waits=fw, inc="SC", ev="a1_fme%d_%d" % (t, ic))
                for cgi, (cgo, cgw) in enumerate(CGS):
                    for ic in range(8):
                        mw = []
                        if ic == 0:
                            mw = ["a1_fme%d_7" % t] + (["wqgT7"] if t == 0 else []) + \
                                 ([("SC", S.ev.get("a1_sil%d_%d" % (t, cgi - 4), 0))]
                                  if cgi >= 4 else
                                  [S.gev("a1_sil%d_%d" % (t - 1, cgi + 2), "SC")])
                        S.add("pe", lambda pe, b=b, ic=ic, cgo=cgo, cgw=cgw, cgi=cgi: pe.matmul(
                            g_ps[cgi % 4][:, 0:cgw], xqf[b][:, P * ic:P * (ic + 1)],
                            wqgT[:, IFF * ic + cgo:IFF * ic + cgo + cgw],
                            start=(ic == 0), stop=(ic == 7)),
                            waits=mw, inc="PE",
                            ev=("a1_g%d_%d" % (t, cgi)) if ic == 7 else None)
                    gw = ["a1_g%d_%d" % (t, cgi)]
                    if cgi == 0 and t >= 2:
                        gw.append("a1_gst%d" % (t - 2))
                    S.add("sc", lambda sc, b=b, cgo=cgo, cgw=cgw, cgi=cgi: sc.activation(
                        gate_t[b][:, cgo:cgo + cgw], g_ps[cgi % 4][:, 0:cgw], AF.Silu),
                        waits=gw, inc="SC", ev="a1_sil%d_%d" % (t, cgi))
            a1_stores(NT - 1)

        S.mark("p3a1")
        with ExitStack() as pa2:
            eb = pa2.enter_context
            wquT = eb(nc.sbuf_tensor("wquT", [P, 8 * IFF], BF16))
            qtb = [make_quant_tiles(eb, "b%d" % i, 704) for i in range(2)]
            xqu = [eb(nc.sbuf_tensor("xqu%d" % i, [P, D], BF16)) for i in range(2)]
            gld = [eb(nc.sbuf_tensor("gld%d" % i, [P, IFF], F32)) for i in range(2)]
            up_t = [eb(nc.sbuf_tensor("up%d" % i, [P, IFF], F32)) for i in range(2)]
            hq = [eb(nc.sbuf_tensor("hq%d" % i, [P, IFF], BF16)) for i in range(2)]
            hqf = [eb(nc.sbuf_tensor("hqf%d" % i, [P, IFF], BF16)) for i in range(2)]
            u_ps = [eb(nc.psum_tensor("ups%d" % i, [P, 512], F32)) for i in range(4)]
            xp4_t = [eb(nc.psum_tensor("xp4t%d" % i, [P, P], BF16)) for i in range(2)]
            xp4_ps = [xp4_t[0][:], xp4_t[1][:]]
            for ic in range(8):
                S.add("sy", lambda sy, ic=ic: sy.dma_start_transpose(
                    out=wquT[:, IFF * ic:IFF * (ic + 1)],
                    in_=g["wqu_h"][:, P * ic:P * (ic + 1)]),
                    waits=["wq_st43", "wq_st42"] +
                          (["a1_g31_5", "a1_sil31_5", "a1_gst31", "a1_gst30",
                            "a1_xqst31", "a1_xqst30", "a1_q31_1"] if ic == 0 else []),
                    inc="TU", ev="wquT%d" % ic)
            def a2_front(t):
                b = t % 2
                S.add("sy", lambda sy, t=t, b=b: sy.dma_start(
                    out=xqu[b][:], in_=g["xqT_h"][P * t:P * (t + 1), :]),
                    waits=["a1_xqst%d" % t, S.gev("a2_u%d_5" % (t - 2), "PE")],
                    inc="XU%d" % b, ev="a2_xq%d" % t)
                S.add("sy", lambda sy, t=t, b=b: sy.dma_start(
                    out=gld[b][:], in_=g["gate_h"][P * t:P * (t + 1), :]),
                    waits=["a1_gst%d" % t, S.gev("a2_q%d_3" % (t - 2), "V")],
                    inc="GL%d" % b, ev="a2_g%d" % t)
                for cgi, (cgo, cgw) in enumerate(CGS):
                    for ic in range(8):
                        mw = []
                        if ic == 0:
                            mw = ["a2_xq%d" % t] + (["wquT7"] if t == 0 else []) + \
                                 ([("SC", S.ev.get("a2_ue%d_%d" % (t, cgi - 4), 0))]
                                  if cgi >= 4 else
                                  [S.gev("a2_ue%d_%d" % (t - 1, cgi + 4 if cgi < 2
                                                         else cgi), "SC")])
                        S.add("pe", lambda pe, b=b, ic=ic, cgo=cgo, cgw=cgw, cgi=cgi: pe.matmul(
                            u_ps[cgi % 4][:, 0:cgw], xqu[b][:, P * ic:P * (ic + 1)],
                            wquT[:, IFF * ic + cgo:IFF * ic + cgo + cgw],
                            start=(ic == 0), stop=(ic == 7)),
                            waits=mw, inc="PE",
                            ev=("a2_u%d_%d" % (t, cgi)) if ic == 7 else None)
                    uw_ = ["a2_u%d_%d" % (t, cgi)]
                    if cgi == 0:
                        uw_.append(S.gev("a2_h%d" % (t - 2), "V"))
                    S.add("sc", lambda sc, b=b, cgo=cgo, cgw=cgw, cgi=cgi: sc.copy(
                        up_t[b][:, cgo:cgo + cgw], u_ps[cgi % 4][:, 0:cgw]),
                        waits=uw_, inc="SC", ev="a2_ue%d_%d" % (t, cgi))
                S.add("v", lambda v, b=b: v.tensor_tensor(gld[b][:], gld[b][:], up_t[b][:],
                                                          op.mult),
                      waits=["a2_g%d" % t, "a2_ue%d_5" % t], inc="V", ev="a2_h%d" % t)
                gens = [quant_gen(qtb[un % 2], gld[b][:, 704 * un:704 * (un + 1)],
                                  hq[b][:, 704 * un:704 * (un + 1)], 704,
                                  ["a2_h%d" % t, S.gev("a2_fme%d_21" % (t - 2), "SC")] +
                                  ([S.gev("a2_q%d_%d" % (t, un - 2), "V")] if un >= 2 else []),
                                  "a2_q%d_%d" % (t, un))
                        for un in range(4)]
                run_gens(gens[0:2])
                run_gens(gens[2:4])

            def a2_back(t):
                b = t % 2
                for ic in range(22):
                    tw = ["a2_q%d_3" % t, "a2_q%d_2" % t]
                    if ic < 2:
                        tw.append(S.gev("a2_fme%d_%d" % (t - 1, ic + 20), "SC"))
                    else:
                        tw.append("a2_fme%d_%d" % (t, ic - 2))
                    S.add("pe", lambda pe, b=b, ic=ic: pe.transpose(
                        xp4_ps[ic % 2], hq[b][:, P * ic:P * (ic + 1)], ident_b[:]),
                        waits=tw, inc="PE", ev="a2_tp%d_%d" % (t, ic))
                    fw = ["a2_tp%d_%d" % (t, ic)]
                    if ic == 0 and t >= 2:
                        fw.append("a2_hst%d" % (t - 2))
                    S.add("sc", lambda sc, b=b, ic=ic: sc.copy(
                        hqf[b][:, P * ic:P * (ic + 1)], xp4_ps[ic % 2]),
                        waits=fw, inc="SC", ev="a2_fme%d_%d" % (t, ic))

            def a2_store(t):
                S.add("sy", lambda sy, t=t: sy.dma_start(
                    out=g["hqT_h"][P * t:P * (t + 1), :], in_=hqf[t % 2][:]),
                    waits=["a2_fme%d_21" % t], inc="SH%d" % (t % 2), ev="a2_hst%d" % t)

            for t in range(NT + 2):
                if t < NT:
                    a2_front(t)
                if t >= 2:
                    a2_store(t - 2)
                if 1 <= t <= NT:
                    a2_back(t - 1)

        S.mark("p3a2")
        with ExitStack() as pb:
            ebb = pb.enter_context
            wqdT = ebb(nc.sbuf_tensor("wqdT", [P, 22 * D], BF16))
            hln = [ebb(nc.sbuf_tensor("hln%d" % i, [P, IFF], BF16)) for i in range(2)]
            sln = [ebb(nc.sbuf_tensor("sln%d" % i, [P, D], BF16)) for i in range(2)]
            sfl = ebb(nc.sbuf_tensor("sfl", [P, D], F32))
            gfl = ebb(nc.sbuf_tensor("gfl", [P, D], F32))
            otl = [ebb(nc.sbuf_tensor("otl%d" % i, [P, D], F32)) for i in range(2)]
            d_ps = [ebb(nc.psum_tensor("d_ps%d" % i, [P, 512], F32)) for i in range(4)]
            for ic in range(22):
                S.add("sy", lambda sy, ic=ic: sy.dma_start_transpose(
                    out=wqdT[:, D * ic:D * (ic + 1)],
                    in_=g["wqd_h"][:, P * ic:P * (ic + 1)]),
                    waits=["wq_st51", "wq_st50"] +
                          (["a2_u31_5", "a2_hst31", "a2_hst30", "a2_q31_3",
                            "a2_fme31_21"] if ic == 0 else []),
                    inc="TD", ev="wqdT%d" % ic)
            def b_store(t):
                S.add("sy", lambda sy, t=t: sy.dma_start(
                    out=g["out"][P * t:P * (t + 1), :], in_=otl[t % 2][:]),
                    waits=["b_out%d" % t], inc="SO%d" % (t % 2), ev="b_ost%d" % t)

            for t in range(NT):
                b = t % 2
                S.add("sy", lambda sy, t=t, b=b: sy.dma_start(
                    out=hln[b][:], in_=g["hqT_h"][P * t:P * (t + 1), :]),
                    waits=["a2_hst%d" % t, S.gev("b_dn%d" % (t - 2), "PE")],
                    inc="HL%d" % b, ev="b_h%d" % t)
                S.add("sy", lambda sy, t=t, b=b: sy.dma_start(
                    out=sln[b][:], in_=g["s_h"][P * t:P * (t + 1), :]),
                    waits=[S.gev("b_sf%d" % (t - 2), "SC"), "p2_sst%d" % t],
                    inc="SL%d" % b, ev="b_s%d" % t)
                ldw = [S.gev("b_out%d" % (t - 2), "V")]
                if t < 2:
                    ldw.append("a1_xn%d" % (30 + t))
                S.add("sy", lambda sy, t=t, b=b: sy.dma_start(
                    out=xa[b][:], in_=g["x"][P * t:P * (t + 1), :]),
                    waits=ldw, inc="XA%d" % b, ev="b_x%d" % t)
                if t >= 1:
                    b_store(t - 1)
                for hh in range(2):
                    for ic in range(22):
                        mw = []
                        if ic == 0:
                            mw = ["b_h%d" % t, S.gev("b_gf%d" % (t - 2), "SC")] + \
                                 (["wqdT21"] if t == 0 else [])
                        S.add("pe", lambda pe, b=b, t=t, ic=ic, hh=hh: pe.matmul(
                            d_ps[2 * (t % 2) + hh][:], hln[b][:, P * ic:P * (ic + 1)],
                            wqdT[:, D * ic + 512 * hh:D * ic + 512 * (hh + 1)],
                            start=(ic == 0), stop=(ic == 21)),
                            waits=mw, inc="PE",
                            ev=("b_dn%d" % t) if (ic == 21 and hh == 1) else None)
                for hh in range(2):
                    S.add("sc", lambda sc, t=t, hh=hh: sc.activation(
                        gfl[:, 512 * hh:512 * (hh + 1)], d_ps[2 * (t % 2) + hh][:], AF.Silu),
                        waits=["b_dn%d" % t] +
                              ([S.gev("b_out%d" % (t - 1), "V")] if hh == 0 else []),
                        inc="SC", ev="b_gf%d" % t if hh == 1 else None)
                S.add("sc", lambda sc, b=b: sc.copy(sfl[:], sln[b][:]),
                      waits=["b_s%d" % t, S.gev("b_t%d" % (t - 1), "V")],
                      inc="SC", ev="b_sf%d" % t)
                S.add("v", lambda v: v.tensor_tensor(gfl[:], gfl[:], sfl[:], op.mult),
                      waits=["b_gf%d" % t, "b_sf%d" % t], inc="V", ev="b_t%d" % t)
                S.add("v", lambda v, b=b: v.tensor_tensor(otl[b][:], gfl[:], xa[b][:], op.add),
                      waits=["b_t%d" % t, "b_x%d" % t] +
                            (["b_ost%d" % (t - 2)] if t >= 2 else []),
                      inc="V", ev="b_out%d" % t)
            b_store(NT - 1)

        import os
        _tr = os.environ.get("KTRUNC", "")
        if _tr:
            mk_ops, mk_cnt = S.marks[_tr]
            for k in S.ops:
                S.ops[k] = S.ops[k][:mk_ops[k]]
            S.cnt = mk_cnt
        S.add("sy", lambda sy: None,
              waits=[(nm, v) for nm, v in sorted(S.cnt.items()) if v > 0])

        for nm in sorted(S.cnt):
            if nm not in sems:
                sems[nm] = ec(nc.semaphore(nm))
        for opslist in S.ops.values():
            for _fn, _w, _inc, _amt in opslist:
                for _nm, _v in _w:
                    if _nm not in sems:
                        sems[_nm] = ec(nc.semaphore(_nm))
        with nc.Block() as block:
            def emit(key):
                def body(eng):
                    for fn, waits, inc, amt in S.ops[key]:
                        for smn, val in waits:
                            eng.wait_ge(sems[smn], val)
                        ins = fn(eng)
                        if inc is not None and ins is not None:
                            ins.then_inc(sems[inc], amt)
                return body
            block.sync(emit("sy"))
            block.gpsimd(emit("gp"))
            block.vector(emit("v"))
            block.scalar(emit("sc"))
            block.tensor(emit("pe"))

    return nc


def kernel(**inputs):
    x = np.ascontiguousarray(np.asarray(inputs["L0_attn_x"], dtype=np.float32))
    B, N, C = x.shape
    shards = [np.ascontiguousarray(x[2 * i:2 * i + 2].reshape(2 * N, C)) for i in range(8)]
    weights = {k: np.ascontiguousarray(np.asarray(v, dtype=np.float32))
               for k, v in inputs.items() if k != "L0_attn_x"}
    nc = build()
    in_maps = [dict(weights, x=s) for s in shards]
    res = run_bass_kernel_spmd(nc, in_maps, core_ids=list(range(8)))
    out = np.concatenate([r["out"].reshape(2, N, C) for r in res.results], axis=0)
    return out.astype(np.float32)


# version 23
# speedup vs baseline: 1.0326x; 1.0326x over previous
import numpy as np
import ml_dtypes
import concourse.bass as bass
import concourse.mybir as mybir
from concourse.bass_utils import run_bass_kernel_spmd
from concourse.alu_op_type import AluOpType as op
from contextlib import ExitStack

F32 = mybir.dt.float32
BF16 = mybir.dt.bfloat16
I32 = mybir.dt.int32
AF = mybir.ActivationFunctionType
AX = mybir.AxisListType.X

P = 128
NT = 32
D = 1024
HID = 2048
IFF = 2816
C32 = float(np.float32(np.float64(1024.0) ** 0.5 + 1e-6))


class Sched:
    def __init__(self):
        self.ops = {k: [] for k in ("sy", "gp", "v", "sc", "pe")}
        self.cnt = {"SY": 0, "GP": 0, "ST": 0, "V": 0, "SC": 0, "PE": 0}
        self.ev = {}
        self.evsem = {}

    def add(self, eng, fn, waits=(), inc=None, ev=None):
        w = []
        for item in waits:
            if item is None:
                continue
            if isinstance(item, str):
                w.append((self.evsem[item], self.ev[item]))
            else:
                sm, val = item
                if val > 0:
                    w.append((sm, val))
        amt = 0
        if inc is not None:
            amt = 1 if inc in ("V", "SC", "PE") else 16
            if inc not in self.cnt:
                self.cnt[inc] = 0
            self.cnt[inc] += amt
        self.ops[eng].append((fn, w, inc, amt))
        if ev is not None:
            assert inc is not None
            self.ev[ev] = self.cnt[inc]
            self.evsem[ev] = inc

    def gev(self, name, sem):
        return (self.evsem.get(name, sem), self.ev.get(name, 0))

    def mark(self, name):
        self.marks = getattr(self, "marks", {})
        self.marks[name] = ({k: len(v) for k, v in self.ops.items()}, dict(self.cnt))


S = Sched()


def v_drain():
    S.add("v", lambda v: v.drain())


def rsqrt_nr(ap_in, ap_out, ti, t1, t2, t3, waits=(), ev=None):
    S.add("v", lambda v: v.tensor_scalar(ti, ap_in.bitcast(I32), 17, 0x4B000000,
                                         op.logical_shift_right, op.bitwise_or),
          waits=list(waits))
    v_drain()
    S.add("v", lambda v: v.tensor_scalar(t1, ti.bitcast(F32), -1.0,
                                         float(0x1000000 + 0x5F37), op.mult, op.add))
    v_drain()
    S.add("v", lambda v: v.tensor_scalar(ti, t1.bitcast(I32), 0x7FFFFF, 16,
                                         op.bitwise_and, op.logical_shift_left))
    v_drain()
    yv = ti.bitcast(F32)
    for it in range(4):
        last = it == 3
        S.add("v", lambda v, yv=yv: v.tensor_tensor(t2, yv, yv, op.mult))
        v_drain()
        S.add("v", lambda v: v.tensor_tensor(t3, ap_in, t2, op.mult))
        v_drain()
        S.add("v", lambda v: v.tensor_scalar(t2, t3, -0.5, 1.5, op.mult, op.add))
        v_drain()
        dst = ap_out if last else t1
        S.add("v", lambda v, yv=yv, dst=dst: v.tensor_tensor(dst, yv, t2, op.mult),
              inc="V" if last else None, ev=ev if last else None)
        if not last:
            v_drain()
            yv = t1


def recip_nr(ap_in, ap_out, ti, t1, t2, waits=(), ev=None):
    S.add("v", lambda v: v.tensor_scalar(ti, ap_in.bitcast(I32), 16, 0x4B000000,
                                         op.logical_shift_right, op.bitwise_or),
          waits=list(waits))
    v_drain()
    S.add("v", lambda v: v.tensor_scalar(t1, ti.bitcast(F32), -1.0,
                                         float(0x1000000 + 0x7EF1), op.mult, op.add))
    v_drain()
    S.add("v", lambda v: v.tensor_scalar(ti, t1.bitcast(I32), 0x7FFFFF, 16,
                                         op.bitwise_and, op.logical_shift_left))
    v_drain()
    yv = ti.bitcast(F32)
    for it in range(4):
        last = it == 3
        S.add("v", lambda v, yv=yv: v.tensor_tensor(t2, ap_in, yv, op.mult))
        v_drain()
        S.add("v", lambda v: v.tensor_scalar(t2, t2, -1.0, 2.0, op.mult, op.add))
        v_drain()
        dst = ap_out if last else t1
        S.add("v", lambda v, yv=yv, dst=dst: v.tensor_tensor(dst, yv, t2, op.mult),
              inc="V" if last else None, ev=ev if last else None)
        if not last:
            v_drain()
            yv = t1


def build():
    global S
    S = Sched()
    nc = bass.Bass("TRN2", target_bir_lowering=False)
    g = {}
    g["x"] = nc.dram_tensor("x", [NT * P, D], F32, kind="ExternalInput").ap()
    for nm, shp in [("L0_attn_scale1", [D]), ("L0_attn_scale2", [D]), ("L0_attn_gamma", [D]),
                    ("L0_attn_qkv_w", [3 * D, D]), ("L0_attn_proj_w", [D, D]),
                    ("L0_attn_proj_b", [D]), ("L0_attn_conv1_w", [HID, D]),
                    ("L0_attn_conv1_b", [HID]), ("L0_attn_conv2_w", [D, HID]),
                    ("L0_attn_conv2_b", [D]), ("L0_ffn_w_gate", [IFF, D]),
                    ("L0_ffn_w_up", [IFF, D]), ("L0_ffn_w_down", [D, IFF])]:
        g[nm] = nc.dram_tensor(nm, shp, F32, kind="ExternalInput").ap()
    g["out"] = nc.dram_tensor("out", [NT * P, D], F32, kind="ExternalOutput").ap()
    import os as _os
    _dbg = "ExternalOutput" if _os.environ.get("KDBG") else "Internal"
    for nm, shp in [("qkv_bf", [3 * D, D]), ("proj_bf", [D, D]), ("c1_bf", [HID, D]),
                    ("c2_bf", [D, HID]), ("wqg_h", [IFF, D]), ("wqu_h", [IFF, D]),
                    ("wqd_h", [D, IFF]), ("x2_h", [NT * P, D]), ("s_h", [NT * P, D]),
                    ("xqT_h", [NT * P, D]), ("hqT_h", [NT * P, IFF])]:
        g[nm] = nc.dram_tensor(nm, shp, BF16, kind=_dbg).ap()
    g["gate_h"] = nc.dram_tensor("gate_h", [NT * P, IFF], F32, kind=_dbg).ap()
    g["dbg"] = nc.dram_tensor("dbg", [P, 6 * NT], F32, kind="ExternalOutput").ap()

    with ExitStack() as es:
        ec = es.enter_context
        sems = {k: ec(nc.semaphore(k)) for k in ("SY", "GP", "ST", "V", "SC", "PE")}

        midstack = ExitStack()
        em = midstack.enter_context
        ident_b = ec(nc.sbuf_tensor("ident_b", [P, P], BF16))
        it_i = ec(nc.sbuf_tensor("it_i", [P, P], I32))
        zrow = ec(nc.sbuf_tensor("zrow", [1, 512], BF16))
        thrB = ec(nc.sbuf_tensor("thrB", [P, 3], F32))
        s1_all = ec(nc.sbuf_tensor("s1_all", [P, NT], F32))
        s2_all = ec(nc.sbuf_tensor("s2_all", [P, NT], F32))
        rstd_all = ec(nc.sbuf_tensor("rstd_all", [P, NT], F32))
        mrs_all = ec(nc.sbuf_tensor("mrs_all", [P, NT], F32))
        rstd1_all = ec(nc.sbuf_tensor("rstd1_all", [P, NT], F32))
        ssq2_all = ec(nc.sbuf_tensor("ssq2_all", [P, NT], F32))
        rstd2_all = ec(nc.sbuf_tensor("rstd2_all", [P, NT], F32))
        st1 = ec(nc.sbuf_tensor("st1", [P, NT], F32))
        st2 = ec(nc.sbuf_tensor("st2", [P, NT], F32))
        st3 = ec(nc.sbuf_tensor("st3", [P, NT], F32))
        st4 = ec(nc.sbuf_tensor("st4", [P, NT], F32))
        st5 = ec(nc.sbuf_tensor("st5", [P, NT], F32))
        sti = ec(nc.sbuf_tensor("sti", [P, NT], I32))
        xa = [ec(nc.sbuf_tensor("xa%d" % i, [P, D], F32)) for i in range(2)]
        xn_t = [ec(nc.sbuf_tensor("xn%d" % i, [P, D], F32)) for i in range(2)]
        reps = {nm: em(nc.sbuf_tensor("rep_" + nm, [P, D], F32))
                for nm in ("s1", "s2", "gm", "pb", "c2b")}
        c1b_fm = em(nc.sbuf_tensor("c1b_fm", [P, HID // P], F32))

        for nm, src, dst, csem in [("qkv", "L0_attn_qkv_w", "qkv_bf", "CQ"),
                                   ("proj", "L0_attn_proj_w", "proj_bf", "CP"),
                                   ("c1", "L0_attn_conv1_w", "c1_bf", "C1"),
                                   ("c2", "L0_attn_conv2_w", "c2_bf", "C2")]:
            S.add("gp", lambda g_, s=src, d=dst: g_.dma_start(out=g[d][:, :], in_=g[s][:, :]),
                  inc=csem, ev="cast_" + nm)
        S.add("gp", lambda g_: g_.memset(thrB[:, 0:1], -6.25), inc="GP")
        S.add("gp", lambda g_: g_.memset(thrB[:, 1:2], -12.25), inc="GP")
        S.add("gp", lambda g_: g_.memset(thrB[:, 2:3], -25.0), inc="GP", ev="thrB")
        S.add("gp", lambda g_: g_.iota(out=it_i[:, :], pattern=[[1, P]], base=0,
                                       channel_multiplier=-1), inc="GP", ev="iota_id")
        S.add("gp", lambda g_: g_.memset(zrow[:], 0.0), inc="GP", ev="zrow")
        S.add("v", lambda v: v.tensor_scalar(ident_b[:, :], it_i[:, :], 0, None,
                                             op.is_equal),
              waits=["iota_id"], inc="V", ev="identb")
        for i, nm in enumerate(("L0_attn_scale1", "L0_attn_scale2", "L0_attn_gamma",
                                "L0_attn_proj_b", "L0_attn_conv2_b")):
            S.add("sy", lambda sy, i=i, nm=nm: sy.dma_start(
                out=vrows[i][:], in_=g[nm].rearrange("d -> () d")), inc="VR")
        for oc in range(HID // P):
            S.add("sy", lambda sy, oc=oc: sy.dma_start(
                out=c1b_fm[:, oc:oc + 1],
                in_=g["L0_attn_conv1_b"][P * oc:P * (oc + 1)].rearrange("p -> p ()")),
                inc="VR", ev=("ld_vrows" if oc == HID // P - 1 else None))
        S.add("v", lambda v: v.memset(ones1[:], 1.0), inc="V", ev="ones1")
        with ExitStack() as p0:
            vrows = [p0.enter_context(nc.sbuf_tensor("vrow%d" % i, [1, D], F32)) for i in range(5)]
            ones1 = p0.enter_context(nc.sbuf_tensor("ones1", [1, P], F32))
            rep_ps = p0.enter_context(nc.psum_tensor("rep_ps", [P, 512], F32))
            xsq = [p0.enter_context(nc.sbuf_tensor("xsq%d" % i, [P, D], F32)) for i in range(2)]
            prev = None
            for i, nm in enumerate(("s1", "s2", "gm", "pb", "c2b")):
                for h in range(2):
                    S.add("pe", lambda pe, i=i, h=h: pe.matmul(
                        rep_ps[:], ones1[:], vrows[i][:, 512 * h:512 * (h + 1)],
                        start=True, stop=True),
                        waits=["ones1", "ld_vrows"] + ([prev] if prev else []),
                        inc="PE", ev="repm%d_%d" % (i, h))
                    S.add("v", lambda v, nm=nm, h=h: v.tensor_copy(
                        reps[nm][:, 512 * h:512 * (h + 1)], rep_ps[:]),
                        waits=["repm%d_%d" % (i, h)], inc="V", ev="repc%d_%d" % (i, h))
                    prev = "repc%d_%d" % (i, h)
            for t in range(NT):
                b = t % 2
                S.add("sy", lambda sy, t=t, b=b: sy.dma_start(
                    out=xa[b][:], in_=g["x"][P * t:P * (t + 1), :]),
                    waits=[S.gev("p05v%d" % (t - 2), "V")], inc="XA%d" % b, ev="p05x%d" % t)
                S.add("sc", lambda sc, b=b: sc.activation(xsq[b][:], xa[b][:], AF.Square),
                      waits=["p05x%d" % t, S.gev("p05v%d" % (t - 2), "V")],
                      inc="SC", ev="p05sq%d" % t)
                S.add("v", lambda v, t=t, b=b: v.reduce_sum(s1_all[:, t:t + 1], xa[b][:], AX),
                      waits=["p05x%d" % t], inc="V")
                S.add("v", lambda v, t=t, b=b: v.reduce_sum(s2_all[:, t:t + 1], xsq[b][:], AX),
                      waits=["p05sq%d" % t], inc="V", ev="p05v%d" % t)
            dbg_sb = p0.enter_context(nc.sbuf_tensor("dbg_sb", [P, 6 * NT], F32))
            S.add("v", lambda v: v.tensor_copy(dbg_sb[:, 0:NT], s1_all[:]),
                  waits=["p05v%d" % (NT - 1)], inc="V", ev="dbgc0")
            S.add("v", lambda v: v.tensor_copy(dbg_sb[:, NT:2 * NT], s2_all[:]), inc="V", ev="dbgc1")
            S.add("v", lambda v: v.tensor_scalar(st1[:], s1_all[:], 1.0 / D, None, op.mult))
            v_drain()
            S.add("v", lambda v: v.tensor_tensor(st2[:], st1[:], st1[:], op.mult))
            v_drain()
            S.add("v", lambda v: v.scalar_tensor_tensor(st2[:], s2_all[:], 1.0 / D, st2[:],
                                                        op.mult, op.subtract))
            v_drain()
            S.add("v", lambda v: v.tensor_scalar(st3[:], st2[:], 1.0, 1e-5, op.mult, op.add))
            v_drain()
            rsqrt_nr(st3[:], rstd_all[:], sti[:], st4[:], st5[:], s1_all[:])
            v_drain()
            S.add("v", lambda v: v.scalar_tensor_tensor(mrs_all[:], st1[:], -1.0, rstd_all[:],
                                                        op.mult, op.mult))
            v_drain()
            S.add("v", lambda v: v.tensor_tensor(st1[:], rstd_all[:], rstd_all[:], op.mult))
            v_drain()
            S.add("v", lambda v: v.tensor_scalar(st2[:], st2[:], float(D), None, op.mult))
            v_drain()
            S.add("v", lambda v: v.tensor_tensor(st3[:], st1[:], st2[:], op.mult))
            v_drain()
            rsqrt_nr(st3[:], rstd1_all[:], sti[:], st4[:], st5[:], s1_all[:])
            v_drain()
            S.add("v", lambda v: v.tensor_scalar(rstd1_all[:], rstd1_all[:], C32, None, op.mult),
                  inc="V", ev="stats_done")
            for j, tile in ((2, rstd_all), (3, mrs_all), (4, rstd1_all), (5, st2)):
                S.add("v", lambda v, j=j, tile=tile: v.tensor_copy(
                    dbg_sb[:, NT * j:NT * (j + 1)], tile[:]), waits=["stats_done"],
                    inc="V", ev="dbgc%d" % j)
            S.add("sy", lambda sy: sy.dma_start(out=g["dbg"][:, :], in_=dbg_sb[:]),
                  waits=["dbgc5"], inc="ST", ev="dbg_st")

        S.mark("p0")
        with ExitStack() as p1:
            e1 = p1.enter_context
            qkvT = e1(nc.sbuf_tensor("qkvT", [P, 8 * 3 * D], BF16))
            projT = e1(nc.sbuf_tensor("projT", [P, 8 * D], BF16))
            xn1_t = [e1(nc.sbuf_tensor("xn1%d" % i, [P, D], BF16)) for i in range(2)]
            fm_sup = [e1(nc.sbuf_tensor("fmsup%d" % i, [P, 8 * 512], BF16)) for i in range(2)]
            kphi = [e1(nc.sbuf_tensor("kphi%d" % i, [P, D], BF16)) for i in range(2)]
            vext = [e1(nc.sbuf_tensor("vext%d" % i, [P, 8 * 129], BF16)) for i in range(2)]
            sq_s0 = e1(nc.sbuf_tensor("sqs0", [P, 512], F32))
            sq_s1 = e1(nc.sbuf_tensor("sqs1", [P, 512], F32))
            sq_s = [sq_s0, sq_s1]
            phi_q = e1(nc.sbuf_tensor("phi_q", [P, 8 * 2048], BF16))
            kv_sb = e1(nc.sbuf_tensor("kv_sb", [P, 8 * 129], BF16))
            kv_bd = e1(nc.sbuf_tensor("kv_bd", [P, 8 * P], BF16))
            bd = e1(nc.sbuf_tensor("bd", [P, 16], BF16))
            zq_sb = e1(nc.sbuf_tensor("zq_sb", [P, 256], F32))
            z_tm = e1(nc.sbuf_tensor("z_tm", [P, 256], F32))
            zn1 = e1(nc.sbuf_tensor("zn1", [P, 256], F32))
            zn2 = e1(nc.sbuf_tensor("zn2", [P, 256], F32))
            zni = e1(nc.sbuf_tensor("zni", [P, 256], I32))
            atz = e1(nc.sbuf_tensor("atz", [P, D], BF16))
            atf = e1(nc.sbuf_tensor("atf", [P, D], BF16))
            x2_t = [e1(nc.sbuf_tensor("x2t%d" % i, [P, D], BF16)) for i in range(2)]
            x2w = e1(nc.sbuf_tensor("x2w", [P, D], F32))
            xp_t2 = [e1(nc.psum_tensor("xpt%d" % i, [P, P], BF16)) for i in range(2)]
            xp_ps = [xp_t2[0][:], xp_t2[1][:]]

            for ic in range(8):
                S.add("sy", lambda sy, ic=ic: sy.dma_start_transpose(
                    out=qkvT[:, 3 * D * ic:3 * D * (ic + 1)],
                    in_=g["qkv_bf"][:, P * ic:P * (ic + 1)]),
                    waits=["cast_qkv"] + (["stats_done", "dbg_st"] if ic == 0 else []),
                    inc="TQ", ev="qkvT%d" % ic)
            for ic in range(8):
                S.add("sy", lambda sy, ic=ic: sy.dma_start_transpose(
                    out=projT[:, D * ic:D * (ic + 1)],
                    in_=g["proj_bf"][:, P * ic:P * (ic + 1)]),
                    waits=["cast_proj"], inc="TP", ev="projT%d" % ic)
            for i in range(2):
                S.add("v", lambda v, i=i: v.memset(
                    vext[i][:].rearrange("p (a c) -> p a c", c=129)[:, :, 128:129], 1.0),
                    inc="V", ev="vext_ones%d" % i)
            S.add("v", lambda v: v.memset(kv_bd[:], 0.0), inc="V", ev="kvbd_zero")

            S.mark("p1pro")
            for seq in range(2):
                t0 = seq * 16
                with ExitStack() as sA:
                    eA = sA.enter_context
                    q_ps = [eA(nc.psum_tensor("q_ps%d" % seq, [P, 512], F32))] * 2
                    kvg_ps = [eA(nc.psum_tensor("kvg_ps%d_%d" % (seq, i), [P, 512], F32))
                              for i in range(2)]
                    kvA = eA(nc.psum_tensor("kvA%d" % seq, [P, 3 * 129], F32))
                    kvB = eA(nc.psum_tensor("kvB%d" % seq, [P, 3 * 129], F32))
                    kvCz = eA(nc.psum_tensor("kvCz%d" % seq, [P, 512], F32))
                    kvC = kvCz[:, 0:2 * 129]

                    def kv_slot(p_):
                        if p_ < 3:
                            return kvA[:], 129 * p_
                        if p_ < 6:
                            return kvB[:], 129 * (p_ - 3)
                        return kvC, 129 * (p_ - 6)

                    if seq == 0:
                        zmw = ["zrow", S.gev("repc4_1", "V")]
                    else:
                        zmw = ["b0_kvsb2", "b0_zev15",
                               S.gev("c_atz%d_1" % 15, "V"), S.gev("c_x2%d" % 15, "V")]
                    for zi, (zt, zn) in enumerate(((kvA, 387), (kvB, 387), (kvCz, 512))):
                        S.add("pe", lambda pe, zt=zt, zn=zn: pe.matmul(
                            zt[:, 0:zn], zrow[0:1, 0:P], zrow[0:1, 0:zn],
                            start=True, stop=False),
                            waits=zmw if zi == 0 else [], inc="PE")

                    def a_part1(t):
                        tt_ = t - t0
                        b = t % 2
                        ldw = [S.gev("a_xn%d" % (t - 2), "SC")]
                        if t < 2:
                            ldw.append("stats_done")
                        if seq == 1 and tt_ < 2:
                            ldw.append(S.gev("c_xn%d" % (t0 - 2 + tt_), "SC"))
                        S.add("sy", lambda sy, t=t, b=b: sy.dma_start(
                            out=xa[b][:], in_=g["x"][P * t:P * (t + 1), :]),
                            waits=ldw, inc="XA%d" % b, ev="a_x%d" % t)
                        S.add("sc", lambda sc, t=t, b=b: sc.activation(
                            xn_t[b][:], xa[b][:], AF.Identity,
                            bias=mrs_all[:, t:t + 1], scale=rstd_all[:, t:t + 1]),
                            waits=["a_x%d" % t, S.gev("a_xn1%d" % (t - 2), "V")] +
                                  (["stats_done"] if t < 2 else []),
                            inc="SC", ev="a_xn%d" % t)
                        xw = ["a_xn%d" % t, S.gev("a_fme%d_7" % (t - 2), "SC")]
                        if t == 0:
                            xw.append("repc0_1")
                        if tt_ % 4 == 0:
                            xw.append(S.gev("a_q%d_7" % (t - 5), "PE"))
                        S.add("v", lambda v, t=t, b=b: v.scalar_tensor_tensor(
                            xn1_t[b][:], xn_t[b][:], rstd1_all[:, t:t + 1], reps["s1"][:],
                            op.mult, op.mult),
                            waits=xw, inc="V", ev="a_xn1%d" % t)

                    a_part1(t0)
                    for tt in range(16):
                        t = t0 + tt
                        b = t % 2
                        sp = tt % 4
                        sup = fm_sup[(tt // 4) % 2]
                        for ic in range(8):
                            tw = ["a_xn1%d" % t]
                            if ic < 2:
                                tw.append(S.gev("a_fme%d_%d" % (t - 1, ic + 6), "SC"))
                            else:
                                tw.append("a_fme%d_%d" % (t, ic - 2))
                            if t == 0 and ic < 2:
                                tw.append("identb")
                            S.add("pe", lambda pe, b=b, ic=ic: pe.transpose(
                                xp_ps[ic % 2], xn1_t[b][:, P * ic:P * (ic + 1)], ident_b[:]),
                                waits=tw, inc="PE", ev="a_tp%d_%d" % (t, ic))
                            S.add("sc", lambda sc, sup=sup, sp=sp, ic=ic: sc.copy(
                                sup[:, 512 * ic + P * sp:512 * ic + P * (sp + 1)],
                                xp_ps[ic % 2]),
                                waits=["a_tp%d_%d" % (t, ic)], inc="SC",
                                ev="a_fme%d_%d" % (t, ic))
                        if tt < 15:
                            a_part1(t + 1)
                        for gg in range(4):
                            for ic in range(8):
                                mw = []
                                if ic == 0:
                                    if gg == 0:
                                        mw = ["a_fme%d_7" % t, "qkvT7",
                                              S.gev("a_kvev%d_2" % (t - 1), "SC")]
                                    elif gg == 1:
                                        mw = [S.gev("a_kvev%d_3" % (t - 1), "SC")]
                                    else:
                                        mw = ["a_ksq%d_%d" % (t, gg - 2)]
                                S.add("pe", lambda pe, sup=sup, sp=sp, ic=ic, gg=gg: pe.matmul(
                                    kvg_ps[gg % 2][:],
                                    sup[:, 512 * ic + P * sp:512 * ic + P * (sp + 1)],
                                    qkvT[:, 3 * D * ic + D + 512 * gg:
                                         3 * D * ic + D + 512 * (gg + 1)],
                                    start=(ic == 0), stop=(ic == 7)),
                                    waits=mw, inc="PE", ev="a_kv%d_%d" % (t, gg))
                            if gg < 2:
                                war = (S.gev("a_qev%d_%d" % (t - 1, 6 + gg), "SC")
                                       if (t - 1) % 4 == 3 else
                                       S.gev("a_kvev%d_%d" % (t - 1, gg), "SC"))
                                S.add("sc", lambda sc, gg=gg: sc.activation(
                                    sq_s[gg][:], kvg_ps[gg % 2][:], AF.Square),
                                    waits=["a_kv%d_%d" % (t, gg), war,
                                           S.gev("a_kvmm%d" % (t - 2), "PE")], inc="SC",
                                    ev="a_ksq%d_%d" % (t, gg))
                                S.add("sc", lambda sc, b=b, gg=gg: sc.activation(
                                    kphi[b][:, 512 * gg:512 * (gg + 1)], sq_s[gg][:],
                                    AF.Exp, scale=-0.5),
                                    waits=["a_ksq%d_%d" % (t, gg)],
                                    inc="SC", ev="a_kvev%d_%d" % (t, gg))
                            else:
                                S.add("sc", lambda sc, b=b, gg=gg: sc.copy(
                                    vext[b][:].rearrange("p (a c) -> p a c", c=129)
                                    [:, 4 * (gg - 2):4 * (gg - 2) + 4, 0:128],
                                    kvg_ps[gg % 2][:].rearrange("p (a c) -> p a c", c=128)),
                                    waits=["a_kv%d_%d" % (t, gg), "vext_ones%d" % b,
                                           S.gev("a_kvmm%d" % (t - 2), "PE")],
                                    inc="SC", ev="a_kvev%d_%d" % (t, gg))
                        for p_ in range(8):
                            kvp, off = kv_slot(p_)
                            mw = []
                            if p_ == 0:
                                mw = ["a_kvev%d_0" % t, "a_kvev%d_1" % t,
                                      "a_kvev%d_2" % t, "a_kvev%d_3" % t]
                                if tt == 0 and seq == 1:
                                    mw.append("b0_kvsb2")
                            S.add("pe", lambda pe, b=b, p_=p_, kvp=kvp, off=off, tt=tt: pe.matmul(
                                kvp[:, off:off + 129],
                                kphi[b][:, P * p_:P * (p_ + 1)],
                                vext[b][:, 129 * p_:129 * (p_ + 1)],
                                start=False, stop=(tt == 15 and p_ in (2, 5, 7))),
                                waits=mw, inc="PE", ev=("a_kvmm%d" % t) if p_ == 7 else None)
                        if sp == 3:
                            for qc in range(8):
                                for ic in range(8):
                                    mw = []
                                    if ic == 0 and qc >= 1:
                                        mw = [S.gev("a_qsq%d_%d" % (t, qc - 1), "SC")]
                                    elif ic == 0 and qc == 0:
                                        mw = [S.gev("a_qsq%d_7" % (t - 4), "SC")]
                                    S.add("pe", lambda pe, sup=sup, qc=qc, ic=ic: pe.matmul(
                                        q_ps[0][:],
                                        qkvT[:, 3 * D * ic + P * qc:3 * D * ic + P * (qc + 1)],
                                        sup[:, 512 * ic:512 * (ic + 1)],
                                        start=(ic == 0), stop=(ic == 7)),
                                        waits=mw, inc="PE", ev="a_q%d_%d" % (t, qc))
                                qwar = (["a_kvev%d_%d" % (t, qc)] if qc < 2 else
                                        [S.gev("a_qev%d_%d" % (t, qc - 2), "SC")])
                                S.add("sc", lambda sc, qc=qc: sc.activation(
                                    sq_s[qc % 2][:], q_ps[0][:], AF.Square),
                                    waits=["a_q%d_%d" % (t, qc)] + qwar, inc="SC",
                                    ev="a_qsq%d_%d" % (t, qc))
                                S.add("sc", lambda sc, qc=qc, tt=tt: sc.activation(
                                    phi_q[:, 2048 * qc + 512 * (tt // 4):
                                          2048 * qc + 512 * (tt // 4) + 512],
                                    sq_s[qc % 2][:], AF.Exp, scale=-0.5),
                                    waits=["a_qsq%d_%d" % (t, qc)],
                                    inc="SC", ev="a_qev%d_%d" % (t, qc))
                    S.mark("pA%d" % seq)
                    tL = t0 + 15
                    for i, (p_lo, w_) in enumerate([(0, 387), (3, 387), (6, 258)]):
                        kvp = kv_slot(p_lo)[0][:, 0:387 if p_lo < 6 else 258]
                        S.add("v", lambda v, kvp=kvp, w_=w_, i=i: v.tensor_copy(
                            kv_sb[:, 387 * i:387 * i + w_], kvp),
                            waits=["a_kvmm%d" % tL, "a_qev%d_7" % tL],
                            inc="V", ev="b%d_kvsb%d" % (seq, i))
                    for p_ in range(8):
                        kvp, off = kv_slot(p_)
                        S.add("v", lambda v, kvp=kvp, off=off, p_=p_: v.tensor_copy(
                            kv_bd[0:64, P * p_:P * p_ + 64], kvp[0:64, off:off + 64]),
                            waits=(["b%d_kvsb2" % seq, "kvbd_zero"] +
                                   ([S.gev("c_at%d" % 15, "PE")] if seq == 1 else []))
                                  if p_ == 0 else [], inc="V")
                        S.add("v", lambda v, kvp=kvp, off=off, p_=p_: v.tensor_copy(
                            kv_bd[64:128, P * p_ + 64:P * (p_ + 1)], kvp[64:128, off:off + 64]),
                            inc="V", ev="b%d_kvbd%d" % (seq, p_))
                    bw = ["b%d_kvsb2" % seq]
                    if seq == 1:
                        bw.append(S.gev("b0_zmm7_3", "PE"))
                    S.add("v", lambda v: v.memset(bd[:], 0.0), waits=bw,
                          inc="V", ev="b%d_bd0" % seq)
                    for p_ in range(8):
                        kvp, off = kv_slot(p_)
                        S.add("v", lambda v, kvp=kvp, off=off, p_=p_: v.tensor_copy(
                            bd[0:64, 2 * p_:2 * p_ + 1], kvp[0:64, off + 128:off + 129]),
                            waits=["b%d_bd0" % seq] if p_ == 0 else [], inc="V")
                        S.add("v", lambda v, kvp=kvp, off=off, p_=p_: v.tensor_copy(
                            bd[64:128, 2 * p_ + 1:2 * p_ + 2], kvp[64:128, off + 128:off + 129]),
                            inc="V", ev="b%d_bd%d" % (seq, p_ + 1))
                    if True:
                        zden_ps = kvCz[:, 400:416]
                        for tt in range(16):
                            for p_ in range(8):
                                mw = ["b%d_bd8" % seq] if (tt == 0 and p_ == 0) else \
                                     (["b%d_zev%d" % (seq, tt - 1)] if p_ == 0 else [])
                                S.add("pe", lambda pe, p_=p_, tt=tt, zden_ps=zden_ps: pe.matmul(
                                    zden_ps[:, 2 * p_:2 * p_ + 2],
                                    phi_q[:, 2048 * p_ + P * tt:2048 * p_ + P * (tt + 1)],
                                    bd[:, 2 * p_:2 * p_ + 2],
                                    start=True, stop=True),
                                    waits=mw, inc="PE", ev="b%d_zmm%d" % (seq, tt))
                            S.add("v", lambda v, tt=tt, zden_ps=zden_ps: v.tensor_copy(
                                zq_sb[:, 16 * tt:16 * (tt + 1)], zden_ps),
                                waits=["b%d_zmm%d" % (seq, tt)],
                                inc="V", ev="b%d_zev%d" % (seq, tt))
                    zw = ["b%d_zev15" % seq]
                    if seq == 1:
                        zw.append(S.gev("c_atz%d_1" % 15, "V"))
                    S.add("v", lambda v: v.tensor_scalar(zn1[:], zq_sb[:], 1.0, 1e-6,
                                                         op.mult, op.add), waits=zw, inc="V")
                    v_drain()
                    recip_nr(zn1[:], z_tm[:], zni[:], zn2[:], zq_sb[:], ev="b%d_z" % seq)
                S.mark("pB%d" % seq)
                with ExitStack() as sC:
                    eC = sC.enter_context
                    atps = [eC(nc.psum_tensor("at%d_%d" % (seq, i), [P, 512], F32))
                            for i in range(4)]
                    prA = eC(nc.psum_tensor("prA%d" % seq, [P, 512], F32))
                    prB = eC(nc.psum_tensor("prB%d" % seq, [P, 512], F32))
                    def c_store(t):
                        S.add("sy", lambda sy, t=t: sy.dma_start(
                            out=g["x2_h"][P * t:P * (t + 1), :], in_=x2_t[t % 2][:]),
                            waits=["c_x2%d" % t], inc="SX%d" % (t % 2), ev="c_x2st%d" % t)

                    def c_part1(t):
                        tt_ = t - t0
                        b = t % 2
                        ldw = [S.gev("c_xn%d" % (t - 2), "SC")]
                        if tt_ < 2:
                            ldw.append("a_xn%d" % (t0 + 14 + tt_))
                        S.add("sy", lambda sy, t=t, b=b: sy.dma_start(
                            out=xa[b][:], in_=g["x"][P * t:P * (t + 1), :]),
                            waits=ldw, inc="XA%d" % b, ev="c_x%d" % t)
                        xnw = ["c_x%d" % t, S.gev("c_x2%d" % (t - 2), "V")]
                        if tt_ < 2:
                            xnw.append(S.gev("a_xn1%d" % (t0 + 14 + tt_), "V"))
                        S.add("sc", lambda sc, t=t, b=b: sc.activation(
                            xn_t[b][:], xa[b][:], AF.Identity,
                            bias=mrs_all[:, t:t + 1], scale=rstd_all[:, t:t + 1]),
                            waits=xnw, inc="SC", ev="c_xn%d" % t)

                    c_part1(t0)
                    for tt in range(16):
                        t = t0 + tt
                        b = t % 2
                        if tt >= 1:
                            c_store(t - 1)
                        for p_ in range(8):
                            ps_h = atps[2 * (t % 2) + (0 if p_ < 4 else 1)]
                            po = (p_ % 4) * P
                            mw = []
                            if p_ == 0:
                                mw = ["b%d_kvbd7" % seq, S.gev("c_atz%d_1" % (t - 2), "V")]
                            S.add("pe", lambda pe, p_=p_, tt=tt, ps_h=ps_h, po=po: pe.matmul(
                                ps_h[:, po:po + P],
                                phi_q[:, 2048 * p_ + P * tt:2048 * p_ + P * (tt + 1)],
                                kv_bd[:, P * p_:P * (p_ + 1)],
                                start=True, stop=True),
                                waits=mw, inc="PE",
                                ev=("c_at%d" % t) if p_ == 7 else None)
                        for hh in range(2):
                            ps_h = atps[2 * (t % 2) + hh]
                            S.add("v", lambda v, hh=hh, ps_h=ps_h: v.tensor_tensor(
                                atz[:, 512 * hh:512 * (hh + 1)]
                                .rearrange("p (h e) -> p h e", e=64),
                                ps_h[:].rearrange("p (h e) -> p h e", e=64),
                                z_tm[:, 16 * tt + 8 * hh:16 * tt + 8 * (hh + 1)]
                                .rearrange("p h -> p h ()").broadcast_to([P, 8, 64]),
                                op.mult),
                                waits=["c_at%d" % t, "b%d_z" % seq] +
                                      ([S.gev("c_atp%d_7" % (t - 1), "PE")] if hh == 0 else []),
                                inc="V", ev="c_atz%d_%d" % (t, hh))
                        for ic in range(8):
                            tw = ["c_atz%d_1" % t]
                            if ic < 2:
                                tw.append(S.gev("c_afe%d_%d" % (t - 1, ic + 6), "SC"))
                            else:
                                tw.append("c_afe%d_%d" % (t, ic - 2))
                            S.add("pe", lambda pe, ic=ic: pe.transpose(
                                xp_ps[ic % 2], atz[:, P * ic:P * (ic + 1)], ident_b[:]),
                                waits=tw, inc="PE", ev="c_atp%d_%d" % (t, ic))
                            S.add("sc", lambda sc, ic=ic: sc.copy(
                                atf[:, P * ic:P * (ic + 1)], xp_ps[ic % 2]),
                                waits=["c_atp%d_%d" % (t, ic),
                                       S.gev("c_proj%d" % (t - 1), "PE") if ic == 0 else None],
                                inc="SC", ev="c_afe%d_%d" % (t, ic))
                        if tt < 15:
                            c_part1(t + 1)
                        for hh, ps_h in enumerate((prA, prB)):
                            for ic in range(8):
                                mw = []
                                if ic == 0:
                                    mw = ["c_afe%d_7" % t] + \
                                         (["projT7"] if t == 0 else []) + \
                                         [S.gev("c_x2%d" % (t - 1), "V")]
                                S.add("pe", lambda pe, ic=ic, hh=hh, ps_h=ps_h: pe.matmul(
                                    ps_h[:], atf[:, P * ic:P * (ic + 1)],
                                    projT[:, D * ic + 512 * hh:D * ic + 512 * (hh + 1)],
                                    start=(ic == 0), stop=(ic == 7)),
                                    waits=mw, inc="PE",
                                    ev=("c_proj%d" % t) if (ic == 7 and hh == 1) else None)
                        for hh, ps_h in enumerate((prA, prB)):
                            sl = slice(512 * hh, 512 * (hh + 1))
                            S.add("v", lambda v, ps_h=ps_h, sl=sl: v.tensor_tensor(
                                x2w[:, sl], ps_h[:], reps["pb"][:, sl], op.add),
                                waits=["c_proj%d" % t] +
                                      (([S.gev("c_x2sq%d" % (t - 1), "SC"),
                                         S.gev("c_ssq%d" % (t - 1), "V")] +
                                        (["c_x2st%d" % (t - 2)] if t >= 2 else []))
                                       if hh == 0 else []),
                                inc="V", ev="c_xq%d_%d" % (t, hh))
                        for hh in range(2):
                            sl = slice(512 * hh, 512 * (hh + 1))
                            S.add("v", lambda v, sl=sl: v.tensor_tensor(
                                x2w[:, sl], x2w[:, sl], reps["gm"][:, sl], op.mult),
                                waits=["c_xq%d_%d" % (t, hh)], inc="V",
                                ev="c_xq%d_%d" % (t, 2 + hh))
                        for hh in range(2):
                            sl = slice(512 * hh, 512 * (hh + 1))
                            S.add("v", lambda v, b=b, sl=sl: v.tensor_tensor(
                                x2_t[b][:, sl], x2w[:, sl], xn_t[b][:, sl], op.add),
                                waits=["c_xq%d_%d" % (t, 2 + hh)] +
                                      (["c_xn%d" % t] if hh == 0 else []), inc="V",
                                ev=("c_x2%d" % t) if hh == 1 else None)
                        S.add("sc", lambda sc, b=b: sc.activation(x2w[:], x2_t[b][:], AF.Square),
                              waits=["c_x2%d" % t, S.gev("c_ssq%d" % (t - 1), "V")],
                              inc="SC", ev="c_x2sq%d" % t)
                        S.add("v", lambda v, t=t: v.reduce_sum(ssq2_all[:, t:t + 1], x2w[:], AX),
                              waits=["c_x2sq%d" % t], inc="V", ev="c_ssq%d" % t)
                    c_store(t0 + 15)

                S.mark("pC%d" % seq)
        def make_quant_tiles(e_, tag, W):
            q = {"W": W}
            for nm in ("u", "u2", "sgn", "s5", "s6", "tA", "tB"):
                q[nm] = e_(nc.sbuf_tensor("q%s_%s" % (tag, nm), [P, W], F32))
            q["s7"] = q["u"]
            nb = W // 16
            for nm in ("am", "sca", "w1", "w2"):
                q[nm] = e_(nc.sbuf_tensor("q%s_%s" % (tag, nm), [P, nb], F32))
            q["rec"] = e_(nc.sbuf_tensor("q%s_rec" % tag, [P, nb], F32))
            q["wi"] = e_(nc.sbuf_tensor("q%s_wi" % tag, [P, nb], I32))
            return q

        _qid = [0]

        def quant_gen(q, src_ap, dst_ap, W, waits_src, done_ev):
            nb = W // 16
            uid = _qid[0]
            _qid[0] += 1
            e = lambda s_: "qz%d_%s" % (uid, s_)
            x3 = src_ap.rearrange("p (nb b) -> p nb b", b=16)
            am, sca, w1, w2, rec, wi = (q[k][:, :nb] for k in ("am", "sca", "w1", "w2", "rec", "wi"))
            u, u2, sgn, s5, s6, s7, tA, tB = (q[k][:, :W] for k in
                                              ("u", "u2", "sgn", "s5", "s6", "s7", "tA", "tB"))
            S.add("v", lambda v: v.tensor_reduce(am, x3, AX, op.max, apply_absolute_value=True),
                  waits=list(waits_src), inc="V", ev=e("am"))
            yield
            S.add("v", lambda v: v.tensor_scalar(sca, am, 1.0 / 6.0, 1e-12, op.mult, op.max),
                  waits=[e("am")], inc="V", ev=e("sca"))
            yield
            S.add("v", lambda v: v.tensor_scalar(wi, sca.bitcast(I32), 16, 0x4B000000,
                                                 op.logical_shift_right, op.bitwise_or),
                  waits=[e("sca")], inc="V", ev=e("wi1"))
            yield
            S.add("v", lambda v: v.tensor_scalar(w1, wi.bitcast(F32), -1.0,
                                                 float(0x1000000 + 0x7EF1), op.mult, op.add),
                  waits=[e("wi1")], inc="V", ev=e("w1a"))
            yield
            S.add("v", lambda v: v.tensor_scalar(wi, w1.bitcast(I32), 0x7FFFFF, 16,
                                                 op.bitwise_and, op.logical_shift_left),
                  waits=[e("w1a")], inc="V", ev=e("wi2"))
            yield
            yv = wi.bitcast(F32)
            prev = e("wi2")
            NRI = 3
            for it in range(NRI):
                S.add("v", lambda v, yv=yv, prev=prev: v.tensor_tensor(w2, sca, yv, op.mult),
                      waits=[prev], inc="V", ev=e("w2a%d" % it))
                yield
                S.add("v", lambda v, it=it: v.tensor_scalar(w2, w2, -1.0, 2.0, op.mult, op.add),
                      waits=[e("w2a%d" % it)], inc="V", ev=e("w2b%d" % it))
                yield
                dst = rec if it == NRI - 1 else w1
                S.add("v", lambda v, yv=yv, dst=dst, it=it: v.tensor_tensor(dst, yv, w2, op.mult),
                      waits=[e("w2b%d" % it)], inc="V", ev=e("nr%d" % it))
                yv = w1
                prev = e("nr%d" % it)
                yield
            rb = rec.rearrange("p nb -> p nb ()").broadcast_to([P, nb, 16])
            S.add("v", lambda v: v.tensor_tensor(
                u.rearrange("p (nb b) -> p nb b", b=16), x3, rb, op.mult),
                waits=[e("nr%d" % (NRI - 1))], inc="V", ev=e("u"))
            yield
            S.add("v", lambda v: v.tensor_tensor(u2, u, u, op.mult),
                  waits=[e("u")], inc="V", ev=e("u2"))
            yield
            S.add("sc", lambda sc: sc.activation(sgn, u, AF.Sign),
                  waits=[e("u")], inc="SC", ev=e("sgn"))
            yield
            S.add("sc", lambda sc: sc.activation(s5, u2, AF.Sign, bias=thrB[:, 0:1]),
                  waits=[e("u2"), "thrB"], inc="SC", ev=e("s5"))
            yield
            S.add("sc", lambda sc: sc.activation(s6, u2, AF.Sign, bias=thrB[:, 1:2]),
                  inc="SC", ev=e("s6"))
            yield
            S.add("sc", lambda sc: sc.activation(s7, u2, AF.Sign, bias=thrB[:, 2:3]),
                  waits=[e("sgn")], inc="SC", ev=e("s7"))
            yield
            S.add("v", lambda v: v.tensor_scalar(tA, u2, 0.0625, None, op.is_gt),
                  waits=[e("u2")], inc="V", ev=e("tA1"))
            yield
            S.add("v", lambda v: v.scalar_tensor_tensor(tB, u2, 0.5625, tA, op.is_gt, op.add),
                  waits=[e("tA1")], inc="V", ev=e("tB1"))
            yield
            S.add("v", lambda v: v.scalar_tensor_tensor(tA, u2, 1.5625, tB, op.is_gt, op.add),
                  waits=[e("tB1")], inc="V", ev=e("tA2"))
            yield
            S.add("v", lambda v: v.scalar_tensor_tensor(tB, u2, 3.0625, tA, op.is_gt, op.add),
                  waits=[e("tA2")], inc="V", ev=e("tB2"))
            yield
            S.add("v", lambda v: v.tensor_tensor(s5, s5, s6, op.add),
                  waits=[e("s5"), e("s6")], inc="V", ev=e("sadd"))
            yield
            S.add("v", lambda v: v.scalar_tensor_tensor(s6, s5, 0.5, s7, op.mult, op.add),
                  waits=[e("s7"), e("sadd")], inc="V", ev=e("smul"))
            yield
            S.add("v", lambda v: v.scalar_tensor_tensor(tA, tB, 0.5, s6, op.mult, op.add),
                  waits=[e("smul")], inc="V", ev=e("tA3"))
            yield
            sb_ = sca.rearrange("p nb -> p nb ()").broadcast_to([P, nb, 16])
            S.add("v", lambda v: v.tensor_tensor(
                tB.rearrange("p (nb b) -> p nb b", b=16),
                sgn.rearrange("p (nb b) -> p nb b", b=16), sb_, op.mult),
                waits=[e("sgn"), e("tA3")], inc="V", ev=e("tBs"))
            yield
            S.add("v", lambda v: v.scalar_tensor_tensor(dst_ap, tA, 2.0, tB, op.add, op.mult),
                  waits=[e("tBs")], inc="V", ev=done_ev)

        def run_gens(gens):
            gens = list(gens)
            while gens:
                nxt = []
                for gn in gens:
                    try:
                        next(gn)
                        nxt.append(gn)
                    except StopIteration:
                        continue
                    else:
                        pass
                gens = nxt



        def wq_chunk_gen(ui, mat, dstm, rc, width, uw, qts, win_t, wq_t, extra_waits):
            slot = ui % 2
            S.add("sy", lambda sy: sy.dma_start(
                out=win_t[:, 0:width], in_=g[mat][P * rc:P * (rc + 1), :]),
                waits=[S.gev("wq_done%d" % (ui - 2), "V")] +
                      (["wq_st%d" % (ui - 2)] if ui >= 2 else []) + list(extra_waits),
                inc="WI%d" % slot, ev="wq_ld%d" % ui)
            yield
            nun = width // uw
            if nun == 2:
                gens = [quant_gen(qts[un], win_t[:, uw * un:uw * (un + 1)],
                                  wq_t[:, uw * un:uw * (un + 1)], uw,
                                  ["wq_ld%d" % ui] +
                                  (["wq_st%d" % (ui - 1)] if ui >= 1 else []) +
                                  [S.gev("wq_q%d_%d" % (ui - 1, un), "V")],
                                  "wq_q%d_%d" % (ui, un))
                        for un in range(2)]
                gens = [g_ for g_ in gens]
                while gens:
                    nxt = []
                    for gn in gens:
                        try:
                            next(gn)
                            nxt.append(gn)
                        except StopIteration:
                            continue
                    gens = nxt
                    yield
                last = "wq_q%d_1" % ui
            else:
                for half in range(2):
                    gens = [quant_gen(qts[un % 2], win_t[:, uw * un:uw * (un + 1)],
                                      wq_t[:, uw * un:uw * (un + 1)], uw,
                                      ["wq_ld%d" % ui] +
                                      (["wq_st%d" % (ui - 2)] if ui >= 2 else []) +
                                      ([S.gev("wq_q%d_%d" % (ui - 1, un % 2 + 2), "V")]
                                       if un < 2 else ["wq_q%d_%d" % (ui, un - 2)]),
                                      "wq_q%d_%d" % (ui, un))
                            for un in (2 * half, 2 * half + 1)]
                    gens = [g_ for g_ in gens]
                    while gens:
                        nxt = []
                        for gn in gens:
                            try:
                                next(gn)
                                nxt.append(gn)
                            except StopIteration:
                                continue
                        gens = nxt
                        yield
                last = "wq_q%d_3" % ui
            S.ev["wq_done%d" % ui] = S.cnt["V"]
            S.evsem["wq_done%d" % ui] = "V"
            S.add("sy", lambda sy: sy.dma_start(
                out=g[dstm][P * rc:P * (rc + 1), :], in_=wq_t[:, 0:width]),
                waits=[last], inc="SW%d" % slot, ev="wq_st%d" % ui)
            yield

        def wq_pump_all(chunks):
            gens = [wq_chunk_gen(*args) for args in chunks]
            prev = None
            for gn in gens:
                try:
                    next(gn)
                except StopIteration:
                    gn = None
                yield
                if prev is not None:
                    for _ in prev:
                        yield
                prev = gn
            if prev is not None:
                for _ in prev:
                    yield

        S.mark("p1")
        with ExitStack() as p2:
            e2 = p2.enter_context
            conv1T = e2(nc.sbuf_tensor("conv1T", [P, 8 * HID], BF16))
            conv2T = e2(nc.sbuf_tensor("conv2T", [P, 16 * D], BF16))
            x2b = [e2(nc.sbuf_tensor("x2b%d" % i, [P, D], BF16)) for i in range(4)]
            x2f = [e2(nc.sbuf_tensor("x2f%d" % i, [P, D], F32)) for i in range(4)]
            xn2_0 = e2(nc.sbuf_tensor("xn2_0", [P, D], BF16))
            xn2 = [xn2_0, xn2_0]
            fm20 = e2(nc.sbuf_tensor("fm2_0", [P, 8 * 512], BF16))
            fm2 = [fm20, fm20]
            hddf0 = e2(nc.sbuf_tensor("hddf0", [P, 16 * 512], BF16))
            hddf = [hddf0, hddf0]
            qtw = [make_quant_tiles(e2, "gw%d" % i, 512) for i in range(2)]
            winw = [e2(nc.sbuf_tensor("winw%d" % i, [P, D], F32)) for i in range(2)]
            wqw0 = e2(nc.sbuf_tensor("wqw0", [P, D], BF16))
            wqw = [wqw0, wqw0]
            gu_chunks = []
            for _ui, (_mat, _dstm) in enumerate(
                    [("L0_ffn_w_gate", "wqg_h")] * 22 + [("L0_ffn_w_up", "wqu_h")] * 22):
                _rc = _ui % 22
                gu_chunks.append((_ui, _mat, _dstm, _rc, D, 512,
                                  qtw, winw[_ui % 2], wqw[_ui % 2], []))
            wq_pump = wq_pump_all(gu_chunks)

            def pump(n):
                for _ in range(n):
                    if next(wq_pump, "END") == "END":
                        break
            s_t0 = e2(nc.sbuf_tensor("s_t0", [P, D], BF16))
            s_t = [s_t0, s_t0]
            scc = e2(nc.sbuf_tensor("scc", [P, 2 * 512], F32))
            c1_ps = [e2(nc.psum_tensor("c1ps%d" % i, [P, 512], F32)) for i in range(2)]
            c2_ps4 = [e2(nc.psum_tensor("c2ps%d" % i, [P, 512], F32)) for i in range(4)]
            xp2_t = [e2(nc.psum_tensor("xp2t%d" % i, [P, P], BF16)) for i in range(2)]
            xp2_ps = [xp2_t[0][:], xp2_t[1][:]]

            for ic in range(8):
                S.add("sy", lambda sy, ic=ic: sy.dma_start_transpose(
                    out=conv1T[:, HID * ic:HID * (ic + 1)],
                    in_=g["c1_bf"][:, P * ic:P * (ic + 1)]),
                    waits=["cast_c1"] + (["c_ssq31", "c_proj31", "c_x2sq31",
                                          "c_x2st31", "c_x2st30"] if ic == 0 else []),
                    inc="T1", ev="c1T%d" % ic)
            for ic in range(16):
                S.add("sy", lambda sy, ic=ic: sy.dma_start_transpose(
                    out=conv2T[:, D * ic:D * (ic + 1)],
                    in_=g["c2_bf"][:, P * ic:P * (ic + 1)]),
                    waits=["cast_c2"], inc="T2", ev="c2T%d" % ic)
            S.add("v", lambda v: v.drain(), waits=["c_ssq%d" % (NT - 1)])
            rsqrt_nr(ssq2_all[:], rstd2_all[:], sti[:], st1[:], st2[:], st3[:])
            v_drain()
            S.add("v", lambda v: v.tensor_scalar(rstd2_all[:], rstd2_all[:], C32, None, op.mult),
                  inc="V", ev="rstd2_done")

            for sp in range(8):
                sup = fm2[sp % 2]
                hdd = hddf[sp % 2]
                for tt in range(4):
                    t = 4 * sp + tt
                    b4 = t % 4
                    b2 = t % 2
                    S.add("sy", lambda sy, t=t, b4=b4: sy.dma_start(
                        out=x2b[b4][:], in_=g["x2_h"][P * t:P * (t + 1), :]),
                        waits=["c_x2st%d" % t, S.gev("p2_s%d" % (t - 4), "V")],
                        inc="XB%d" % b4, ev="p2_x2%d" % t)
                    S.add("sc", lambda sc, b4=b4: sc.copy(x2f[b4][:], x2b[b4][:]),
                          waits=["p2_x2%d" % t, S.gev("p2_s%d" % (t - 4), "V")],
                          inc="SC", ev="p2_x2f%d" % t)
                    S.add("v", lambda v, t=t, b4=b4, b2=b2: v.scalar_tensor_tensor(
                        xn2[b2][:], x2f[b4][:], rstd2_all[:, t:t + 1], reps["s2"][:],
                        op.mult, op.mult),
                        waits=["p2_x2f%d" % t, "rstd2_done",
                               S.gev("p2_fme%d_7" % (t - 1), "SC")],
                        inc="V", ev="p2_xn2%d" % t)
                    for ic in range(8):
                        tw = ["p2_xn2%d" % t]
                        if ic < 2:
                            tw.append(S.gev("p2_fme%d_%d" % (t - 1, ic + 6), "SC"))
                        else:
                            tw.append("p2_fme%d_%d" % (t, ic - 2))
                        S.add("pe", lambda pe, b2=b2, ic=ic: pe.transpose(
                            xp2_ps[ic % 2], xn2[b2][:, P * ic:P * (ic + 1)], ident_b[:]),
                            waits=tw, inc="PE", ev="p2_tp%d_%d" % (t, ic))
                        fw = ["p2_tp%d_%d" % (t, ic)]
                        if tt == 0 and ic == 0:
                            fw.append(S.gev("p2_c1%d_15" % (sp - 1), "PE"))
                        S.add("sc", lambda sc, sup=sup, tt=tt, ic=ic: sc.copy(
                            sup[:, 512 * ic + P * tt:512 * ic + P * (tt + 1)],
                            xp2_ps[ic % 2]),
                            waits=fw, inc="SC", ev="p2_fme%d_%d" % (t, ic))
                    pump(55)
                tl = 4 * sp + 3
                for oc in range(16):
                    for ic in range(8):
                        mw = []
                        if ic == 0:
                            mw = ["p2_fme%d_7" % tl] + (["c1T7"] if sp == 0 else []) + \
                                 ([("SC", S.ev.get("p2_gel%d_%d" % (sp, oc - 2), 0))]
                                  if oc >= 2 else
                                  [S.gev("p2_gel%d_%d" % (sp - 1, oc + 14), "SC")])
                        S.add("pe", lambda pe, sup=sup, oc=oc, ic=ic: pe.matmul(
                            c1_ps[oc % 2][:],
                            conv1T[:, HID * ic + P * oc:HID * ic + P * (oc + 1)],
                            sup[:, 512 * ic:512 * (ic + 1)],
                            start=(ic == 0), stop=(ic == 7)),
                            waits=mw, inc="PE", ev="p2_c1%d_%d" % (sp, oc))
                    gw = ["p2_c1%d_%d" % (sp, oc)]
                    if oc == 0:
                        gw.append(S.gev("p2_c2%d" % (4 * (sp - 1) + 3), "PE"))
                    S.add("sc", lambda sc, hdd=hdd, oc=oc: sc.activation(
                        hdd[:, 512 * oc:512 * (oc + 1)], c1_ps[oc % 2][:], AF.Gelu,
                        bias=c1b_fm[:, oc:oc + 1]),
                        waits=gw, inc="SC", ev="p2_gel%d_%d" % (sp, oc))
                for tt in range(4):
                    t = 4 * sp + tt
                    b4 = t % 4
                    b2 = t % 2
                    if t >= 1:
                        S.add("sy", lambda sy, t=t: sy.dma_start(
                            out=g["s_h"][P * (t - 1):P * t, :], in_=s_t[(t - 1) % 2][:]),
                            waits=["p2_s%d" % (t - 1)], inc="SS%d" % ((t - 1) % 2),
                            ev="p2_sst%d" % (t - 1))
                    for hh in range(2):
                        for ic in range(16):
                            mw = []
                            if ic == 0:
                                mw = ["p2_gel%d_15" % sp, S.gev("p2_s%d" % (t - 2), "V")] + \
                                     (["c2T15"] if t == 0 else [])
                            S.add("pe", lambda pe, hdd=hdd, t=t, tt=tt, ic=ic, hh=hh: pe.matmul(
                                c2_ps4[2 * (t % 2) + hh][:],
                                hdd[:, 512 * ic + P * tt:512 * ic + P * (tt + 1)],
                                conv2T[:, D * ic + 512 * hh:D * ic + 512 * (hh + 1)],
                                start=(ic == 0), stop=(ic == 15)),
                                waits=mw, inc="PE",
                                ev=("p2_c2%d" % t) if (ic == 15 and hh == 1) else None)
                    for hh in range(2):
                        sl = slice(512 * hh, 512 * (hh + 1))
                        S.add("v", lambda v, t=t, hh=hh, sl=sl: v.tensor_tensor(
                            scc[:, sl], c2_ps4[2 * (t % 2) + hh][:], reps["c2b"][:, sl],
                            op.add),
                            waits=(["p2_c2%d" % t] +
                                   (["p2_sst%d" % (t - 1)] if t >= 1 else []) +
                                   ([S.gev("p2_sq%d_4" % (t - 1), "V")] if t >= 1 else [])
                                   if hh == 0 else
                                   [S.gev("p2_s%d" % (t - 1), "V")]),
                            inc="V", ev="p2_sq%d_%d" % (t, hh))
                    for hh in range(2):
                        sl = slice(512 * hh, 512 * (hh + 1))
                        S.add("v", lambda v, sl=sl: v.tensor_tensor(
                            scc[:, sl], scc[:, sl], reps["gm"][:, sl], op.mult),
                            waits=["p2_sq%d_%d" % (t, hh)], inc="V",
                            ev="p2_sq%d_%d" % (t, 2 + hh))
                    for hh in range(2):
                        sl = slice(512 * hh, 512 * (hh + 1))
                        S.add("v", lambda v, b4=b4, b2=b2, sl=sl, hh=hh: v.tensor_tensor(
                            s_t[b2][:, sl], scc[:, sl], x2f[b4][:, sl], op.add),
                            waits=["p2_sq%d_%d" % (t, 2 + hh)] +
                                  (["p2_sst%d" % (t - 2)] if (hh == 0 and t >= 2) else []),
                            inc="V", ev=("p2_s%d" % t) if hh == 1 else "p2_sq%d_4" % t)
                    pump(25)
            S.add("sy", lambda sy: sy.dma_start(
                out=g["s_h"][P * 31:P * 32, :], in_=s_t[31 % 2][:]),
                waits=["p2_s31"], inc="SS%d" % (31 % 2), ev="p2_sst31")
            pump(10 ** 9)

        S.mark("p2")
        midstack.close()

        with ExitStack() as p25:
            e25 = p25.enter_context
            qt = [make_quant_tiles(e25, "w%d" % i, 704) for i in range(2)]
            win = [e25(nc.sbuf_tensor("win%d" % i, [P, IFF], F32)) for i in range(2)]
            wq = [e25(nc.sbuf_tensor("wq%d" % i, [P, IFF], BF16)) for i in range(2)]
            dn_chunks = []
            for _j in range(8):
                _ui = 44 + _j
                extra = (["p2_s31", "p2_c231", "p2_gel7_15", "p2_sst31", "p2_sst30",
                          "wq_st43", "wq_st42", "wq_done43"] if _j == 0 else [])
                dn_chunks.append((_ui, "L0_ffn_w_down", "wqd_h", _j, IFF, 704,
                                  qt, win[_ui % 2], wq[_ui % 2], extra))
            for _ in wq_pump_all(dn_chunks):
                pass

        S.mark("p25")
        CGS = [(0, 512), (512, 512), (1024, 512), (1536, 512), (2048, 512), (2560, 256)]
        with ExitStack() as pa1:
            ea = pa1.enter_context
            wqgT = ea(nc.sbuf_tensor("wqgT", [P, 8 * IFF], BF16))
            qta = [make_quant_tiles(ea, "a%d" % i, 512) for i in range(2)]
            xq = [ea(nc.sbuf_tensor("xq%d" % i, [P, D], BF16)) for i in range(2)]
            xqf = [ea(nc.sbuf_tensor("xqf%d" % i, [P, D], BF16)) for i in range(2)]
            gate_t = [ea(nc.sbuf_tensor("gate%d" % i, [P, IFF], F32)) for i in range(2)]
            g_ps = [ea(nc.psum_tensor("gps%d" % i, [P, 512], F32)) for i in range(4)]
            xp3_t = [ea(nc.psum_tensor("xp3t%d" % i, [P, P], BF16)) for i in range(2)]
            xp3_ps = [xp3_t[0][:], xp3_t[1][:]]
            for ic in range(8):
                S.add("sy", lambda sy, ic=ic: sy.dma_start_transpose(
                    out=wqgT[:, IFF * ic:IFF * (ic + 1)],
                    in_=g["wqg_h"][:, P * ic:P * (ic + 1)]),
                    waits=["wq_st21", "wq_st20"] +
                          (["wq_st51", "wq_st50", "wq_done51"] if ic == 0 else []),
                    inc="TG", ev="wqgT%d" % ic)
            def a1_stores(t):
                S.add("sy", lambda sy, t=t: sy.dma_start(
                    out=g["xqT_h"][P * t:P * (t + 1), :], in_=xqf[t % 2][:]),
                    waits=["a1_fme%d_7" % t], inc="SQ%d" % (t % 2), ev="a1_xqst%d" % t)
                S.add("sy", lambda sy, t=t: sy.dma_start(
                    out=g["gate_h"][P * t:P * (t + 1), :], in_=gate_t[t % 2][:]),
                    waits=["a1_sil%d_5" % t], inc="SG%d" % (t % 2), ev="a1_gst%d" % t)

            for t in range(NT):
                b = t % 2
                ldw = [S.gev("a1_xn%d" % (t - 2), "SC")]
                if t < 2:
                    ldw.append("c_xn%d" % (30 + t))
                S.add("sy", lambda sy, t=t, b=b: sy.dma_start(
                    out=xa[b][:], in_=g["x"][P * t:P * (t + 1), :]),
                    waits=ldw, inc="XA%d" % b, ev="a1_x%d" % t)
                if t >= 1:
                    a1_stores(t - 1)
                xnw = ["a1_x%d" % t, S.gev("a1_q%d_1" % (t - 2), "V")]
                S.add("v", lambda v, t=t, b=b: v.tensor_scalar(
                    xn_t[b][:], xa[b][:], rstd_all[:, t:t + 1], mrs_all[:, t:t + 1],
                    op.mult, op.add),
                    waits=xnw, inc="V", ev="a1_xn%d" % t)
                gens = [quant_gen(qta[un], xn_t[b][:, 512 * un:512 * (un + 1)],
                                  xq[b][:, 512 * un:512 * (un + 1)], 512,
                                  ["a1_xn%d" % t,
                                   S.gev("a1_fme%d_7" % (t - 2), "SC")],
                                  "a1_q%d_%d" % (t, un))
                        for un in range(2)]
                run_gens(gens)
                for ic in range(8):
                    tw = ["a1_q%d_0" % t, "a1_q%d_1" % t]
                    if ic < 2:
                        tw.append(S.gev("a1_fme%d_%d" % (t - 1, ic + 6), "SC"))
                    else:
                        tw.append("a1_fme%d_%d" % (t, ic - 2))
                    S.add("pe", lambda pe, b=b, ic=ic: pe.transpose(
                        xp3_ps[ic % 2], xq[b][:, P * ic:P * (ic + 1)], ident_b[:]),
                        waits=tw, inc="PE", ev="a1_tp%d_%d" % (t, ic))
                    fw = ["a1_tp%d_%d" % (t, ic)]
                    if ic == 0:
                        fw += [S.gev("a1_g%d_5" % (t - 2), "PE")] + \
                              (["a1_xqst%d" % (t - 2)] if t >= 2 else [])
                    S.add("sc", lambda sc, b=b, ic=ic: sc.copy(
                        xqf[b][:, P * ic:P * (ic + 1)], xp3_ps[ic % 2]),
                        waits=fw, inc="SC", ev="a1_fme%d_%d" % (t, ic))
                for cgi, (cgo, cgw) in enumerate(CGS):
                    for ic in range(8):
                        mw = []
                        if ic == 0:
                            mw = ["a1_fme%d_7" % t] + (["wqgT7"] if t == 0 else []) + \
                                 ([("SC", S.ev.get("a1_sil%d_%d" % (t, cgi - 4), 0))]
                                  if cgi >= 4 else
                                  [S.gev("a1_sil%d_%d" % (t - 1, cgi + 2), "SC")])
                        S.add("pe", lambda pe, b=b, ic=ic, cgo=cgo, cgw=cgw, cgi=cgi: pe.matmul(
                            g_ps[cgi % 4][:, 0:cgw], xqf[b][:, P * ic:P * (ic + 1)],
                            wqgT[:, IFF * ic + cgo:IFF * ic + cgo + cgw],
                            start=(ic == 0), stop=(ic == 7)),
                            waits=mw, inc="PE",
                            ev=("a1_g%d_%d" % (t, cgi)) if ic == 7 else None)
                    gw = ["a1_g%d_%d" % (t, cgi)]
                    if cgi == 0 and t >= 2:
                        gw.append("a1_gst%d" % (t - 2))
                    S.add("sc", lambda sc, b=b, cgo=cgo, cgw=cgw, cgi=cgi: sc.activation(
                        gate_t[b][:, cgo:cgo + cgw], g_ps[cgi % 4][:, 0:cgw], AF.Silu),
                        waits=gw, inc="SC", ev="a1_sil%d_%d" % (t, cgi))
            a1_stores(NT - 1)

        S.mark("p3a1")
        with ExitStack() as pa2:
            eb = pa2.enter_context
            wquT = eb(nc.sbuf_tensor("wquT", [P, 8 * IFF], BF16))
            qtb = [make_quant_tiles(eb, "b%d" % i, 704) for i in range(2)]
            xqu = [eb(nc.sbuf_tensor("xqu%d" % i, [P, D], BF16)) for i in range(2)]
            gld = [eb(nc.sbuf_tensor("gld%d" % i, [P, IFF], F32)) for i in range(2)]
            up_t = [eb(nc.sbuf_tensor("up%d" % i, [P, IFF], F32)) for i in range(2)]
            hq = [eb(nc.sbuf_tensor("hq%d" % i, [P, IFF], BF16)) for i in range(2)]
            hqf = [eb(nc.sbuf_tensor("hqf%d" % i, [P, IFF], BF16)) for i in range(2)]
            u_ps = [eb(nc.psum_tensor("ups%d" % i, [P, 512], F32)) for i in range(4)]
            xp4_t = [eb(nc.psum_tensor("xp4t%d" % i, [P, P], BF16)) for i in range(2)]
            xp4_ps = [xp4_t[0][:], xp4_t[1][:]]
            for ic in range(8):
                S.add("sy", lambda sy, ic=ic: sy.dma_start_transpose(
                    out=wquT[:, IFF * ic:IFF * (ic + 1)],
                    in_=g["wqu_h"][:, P * ic:P * (ic + 1)]),
                    waits=["wq_st43", "wq_st42"] +
                          (["a1_g31_5", "a1_sil31_5", "a1_gst31", "a1_gst30",
                            "a1_xqst31", "a1_xqst30", "a1_q31_1"] if ic == 0 else []),
                    inc="TU", ev="wquT%d" % ic)
            def a2_front(t):
                b = t % 2
                S.add("sy", lambda sy, t=t, b=b: sy.dma_start(
                    out=xqu[b][:], in_=g["xqT_h"][P * t:P * (t + 1), :]),
                    waits=["a1_xqst%d" % t, S.gev("a2_u%d_5" % (t - 2), "PE")],
                    inc="XU%d" % b, ev="a2_xq%d" % t)
                S.add("sy", lambda sy, t=t, b=b: sy.dma_start(
                    out=gld[b][:], in_=g["gate_h"][P * t:P * (t + 1), :]),
                    waits=["a1_gst%d" % t, S.gev("a2_q%d_3" % (t - 2), "V")],
                    inc="GL%d" % b, ev="a2_g%d" % t)
                for cgi, (cgo, cgw) in enumerate(CGS):
                    for ic in range(8):
                        mw = []
                        if ic == 0:
                            mw = ["a2_xq%d" % t] + (["wquT7"] if t == 0 else []) + \
                                 ([("SC", S.ev.get("a2_ue%d_%d" % (t, cgi - 4), 0))]
                                  if cgi >= 4 else
                                  [S.gev("a2_ue%d_%d" % (t - 1, cgi + 4 if cgi < 2
                                                         else cgi), "SC")])
                        S.add("pe", lambda pe, b=b, ic=ic, cgo=cgo, cgw=cgw, cgi=cgi: pe.matmul(
                            u_ps[cgi % 4][:, 0:cgw], xqu[b][:, P * ic:P * (ic + 1)],
                            wquT[:, IFF * ic + cgo:IFF * ic + cgo + cgw],
                            start=(ic == 0), stop=(ic == 7)),
                            waits=mw, inc="PE",
                            ev=("a2_u%d_%d" % (t, cgi)) if ic == 7 else None)
                    uw_ = ["a2_u%d_%d" % (t, cgi)]
                    if cgi == 0:
                        uw_.append(S.gev("a2_h%d" % (t - 2), "V"))
                    S.add("sc", lambda sc, b=b, cgo=cgo, cgw=cgw, cgi=cgi: sc.copy(
                        up_t[b][:, cgo:cgo + cgw], u_ps[cgi % 4][:, 0:cgw]),
                        waits=uw_, inc="SC", ev="a2_ue%d_%d" % (t, cgi))
                S.add("v", lambda v, b=b: v.tensor_tensor(gld[b][:], gld[b][:], up_t[b][:],
                                                          op.mult),
                      waits=["a2_g%d" % t, "a2_ue%d_5" % t], inc="V", ev="a2_h%d" % t)
                gens = [quant_gen(qtb[un % 2], gld[b][:, 704 * un:704 * (un + 1)],
                                  hq[b][:, 704 * un:704 * (un + 1)], 704,
                                  ["a2_h%d" % t, S.gev("a2_fme%d_21" % (t - 2), "SC")] +
                                  ([S.gev("a2_q%d_%d" % (t, un - 2), "V")] if un >= 2 else []),
                                  "a2_q%d_%d" % (t, un))
                        for un in range(4)]
                run_gens(gens[0:2])
                run_gens(gens[2:4])

            def a2_back(t):
                b = t % 2
                for ic in range(22):
                    tw = ["a2_q%d_3" % t, "a2_q%d_2" % t]
                    if ic < 2:
                        tw.append(S.gev("a2_fme%d_%d" % (t - 1, ic + 20), "SC"))
                    else:
                        tw.append("a2_fme%d_%d" % (t, ic - 2))
                    S.add("pe", lambda pe, b=b, ic=ic: pe.transpose(
                        xp4_ps[ic % 2], hq[b][:, P * ic:P * (ic + 1)], ident_b[:]),
                        waits=tw, inc="PE", ev="a2_tp%d_%d" % (t, ic))
                    fw = ["a2_tp%d_%d" % (t, ic)]
                    if ic == 0 and t >= 2:
                        fw.append("a2_hst%d" % (t - 2))
                    S.add("sc", lambda sc, b=b, ic=ic: sc.copy(
                        hqf[b][:, P * ic:P * (ic + 1)], xp4_ps[ic % 2]),
                        waits=fw, inc="SC", ev="a2_fme%d_%d" % (t, ic))

            def a2_store(t):
                S.add("sy", lambda sy, t=t: sy.dma_start(
                    out=g["hqT_h"][P * t:P * (t + 1), :], in_=hqf[t % 2][:]),
                    waits=["a2_fme%d_21" % t], inc="SH%d" % (t % 2), ev="a2_hst%d" % t)

            for t in range(NT + 2):
                if t < NT:
                    a2_front(t)
                if t >= 2:
                    a2_store(t - 2)
                if 1 <= t <= NT:
                    a2_back(t - 1)

        S.mark("p3a2")
        with ExitStack() as pb:
            ebb = pb.enter_context
            wqdT = ebb(nc.sbuf_tensor("wqdT", [P, 22 * D], BF16))
            hln = [ebb(nc.sbuf_tensor("hln%d" % i, [P, IFF], BF16)) for i in range(2)]
            sln = [ebb(nc.sbuf_tensor("sln%d" % i, [P, D], BF16)) for i in range(2)]
            sfl = ebb(nc.sbuf_tensor("sfl", [P, D], F32))
            gfl = ebb(nc.sbuf_tensor("gfl", [P, D], F32))
            otl = [ebb(nc.sbuf_tensor("otl%d" % i, [P, D], F32)) for i in range(2)]
            d_ps = [ebb(nc.psum_tensor("d_ps%d" % i, [P, 512], F32)) for i in range(4)]
            for ic in range(22):
                S.add("sy", lambda sy, ic=ic: sy.dma_start_transpose(
                    out=wqdT[:, D * ic:D * (ic + 1)],
                    in_=g["wqd_h"][:, P * ic:P * (ic + 1)]),
                    waits=["wq_st51", "wq_st50"] +
                          (["a2_u31_5", "a2_hst31", "a2_hst30", "a2_q31_3",
                            "a2_fme31_21"] if ic == 0 else []),
                    inc="TD", ev="wqdT%d" % ic)
            def b_store(t):
                S.add("sy", lambda sy, t=t: sy.dma_start(
                    out=g["out"][P * t:P * (t + 1), :], in_=otl[t % 2][:]),
                    waits=["b_out%d" % t], inc="SO%d" % (t % 2), ev="b_ost%d" % t)

            for t in range(NT):
                b = t % 2
                S.add("sy", lambda sy, t=t, b=b: sy.dma_start(
                    out=hln[b][:], in_=g["hqT_h"][P * t:P * (t + 1), :]),
                    waits=["a2_hst%d" % t, S.gev("b_dn%d" % (t - 2), "PE")],
                    inc="HL%d" % b, ev="b_h%d" % t)
                S.add("sy", lambda sy, t=t, b=b: sy.dma_start(
                    out=sln[b][:], in_=g["s_h"][P * t:P * (t + 1), :]),
                    waits=[S.gev("b_sf%d" % (t - 2), "SC"), "p2_sst%d" % t],
                    inc="SL%d" % b, ev="b_s%d" % t)
                ldw = [S.gev("b_out%d" % (t - 2), "V")]
                if t < 2:
                    ldw.append("a1_xn%d" % (30 + t))
                S.add("sy", lambda sy, t=t, b=b: sy.dma_start(
                    out=xa[b][:], in_=g["x"][P * t:P * (t + 1), :]),
                    waits=ldw, inc="XA%d" % b, ev="b_x%d" % t)
                if t >= 1:
                    b_store(t - 1)
                for hh in range(2):
                    for ic in range(22):
                        mw = []
                        if ic == 0:
                            mw = ["b_h%d" % t, S.gev("b_gf%d" % (t - 2), "SC")] + \
                                 (["wqdT21"] if t == 0 else [])
                        S.add("pe", lambda pe, b=b, t=t, ic=ic, hh=hh: pe.matmul(
                            d_ps[2 * (t % 2) + hh][:], hln[b][:, P * ic:P * (ic + 1)],
                            wqdT[:, D * ic + 512 * hh:D * ic + 512 * (hh + 1)],
                            start=(ic == 0), stop=(ic == 21)),
                            waits=mw, inc="PE",
                            ev=("b_dn%d" % t) if (ic == 21 and hh == 1) else None)
                for hh in range(2):
                    S.add("sc", lambda sc, t=t, hh=hh: sc.activation(
                        gfl[:, 512 * hh:512 * (hh + 1)], d_ps[2 * (t % 2) + hh][:], AF.Silu),
                        waits=["b_dn%d" % t] +
                              ([S.gev("b_out%d" % (t - 1), "V")] if hh == 0 else []),
                        inc="SC", ev="b_gf%d" % t if hh == 1 else None)
                S.add("sc", lambda sc, b=b: sc.copy(sfl[:], sln[b][:]),
                      waits=["b_s%d" % t, S.gev("b_t%d" % (t - 1), "V")],
                      inc="SC", ev="b_sf%d" % t)
                S.add("v", lambda v: v.tensor_tensor(gfl[:], gfl[:], sfl[:], op.mult),
                      waits=["b_gf%d" % t, "b_sf%d" % t], inc="V", ev="b_t%d" % t)
                S.add("v", lambda v, b=b: v.tensor_tensor(otl[b][:], gfl[:], xa[b][:], op.add),
                      waits=["b_t%d" % t, "b_x%d" % t] +
                            (["b_ost%d" % (t - 2)] if t >= 2 else []),
                      inc="V", ev="b_out%d" % t)
            b_store(NT - 1)

        import os
        _tr = os.environ.get("KTRUNC", "")
        if _tr:
            mk_ops, mk_cnt = S.marks[_tr]
            for k in S.ops:
                S.ops[k] = S.ops[k][:mk_ops[k]]
            S.cnt = mk_cnt
        S.add("sy", lambda sy: None,
              waits=[(nm, v) for nm, v in sorted(S.cnt.items()) if v > 0])

        for nm in sorted(S.cnt):
            if nm not in sems:
                sems[nm] = ec(nc.semaphore(nm))
        for opslist in S.ops.values():
            for _fn, _w, _inc, _amt in opslist:
                for _nm, _v in _w:
                    if _nm not in sems:
                        sems[_nm] = ec(nc.semaphore(_nm))
        with nc.Block() as block:
            def emit(key):
                def body(eng):
                    for fn, waits, inc, amt in S.ops[key]:
                        for smn, val in waits:
                            eng.wait_ge(sems[smn], val)
                        ins = fn(eng)
                        if inc is not None and ins is not None:
                            ins.then_inc(sems[inc], amt)
                return body
            block.sync(emit("sy"))
            block.gpsimd(emit("gp"))
            block.vector(emit("v"))
            block.scalar(emit("sc"))
            block.tensor(emit("pe"))

    return nc


def kernel(**inputs):
    x = np.ascontiguousarray(np.asarray(inputs["L0_attn_x"], dtype=np.float32))
    B, N, C = x.shape
    shards = [np.ascontiguousarray(x[2 * i:2 * i + 2].reshape(2 * N, C)) for i in range(8)]
    weights = {k: np.ascontiguousarray(np.asarray(v, dtype=np.float32))
               for k, v in inputs.items() if k != "L0_attn_x"}
    nc = build()
    in_maps = [dict(weights, x=s) for s in shards]
    res = run_bass_kernel_spmd(nc, in_maps, core_ids=list(range(8)))
    out = np.concatenate([r["out"].reshape(2, N, C) for r in res.results], axis=0)
    return out.astype(np.float32)
